# version 01
# baseline (speedup 1.0000x reference)
import numpy as np
import concourse.bacc as bacc
import concourse.bass as bass
import concourse.mybir as mybir
import concourse.hw_specs as hw_specs
from contextlib import ExitStack

_orig_get_act_tables = hw_specs.get_activation_tables
_AFT = mybir.ActivationFunctionType


def _patched_act_tables(arch):
    out = {}
    for name, funcs in _orig_get_act_tables(arch).items():
        f = set(funcs)
        if name != "natural_log_exp_and_others":
            f.discard(_AFT.Exp)
            f.discard(_AFT.Ln)
        out[name] = f
    return out


hw_specs.get_activation_tables = _patched_act_tables

F32 = mybir.dt.float32
BF16 = mybir.dt.bfloat16
I16 = mybir.dt.int16
U8 = mybir.dt.uint8
U32 = mybir.dt.uint32
ALU = mybir.AluOpType
AF = mybir.ActivationFunctionType
AX = mybir.AxisListType

NB, S, D, H = 4, 2048, 1024, 16
NT = NB * S
NSL = S // 128
CAP = 384
NCH = CAP // 128
DFF = 4096
NEXP = 8
EPS = 1e-5
SCALE = 0.125


def build(debug=False, do_ffn=True, nb=NB):
    nc = bacc.Bacc("TRN2")
    ntok = nb * S
    x_in = nc.dram_tensor("x", [nb, S, D], F32, kind="ExternalInput").ap()
    wq_d = nc.dram_tensor("wq", [D, D], F32, kind="ExternalInput").ap()
    wk_d = nc.dram_tensor("wk", [D, D], F32, kind="ExternalInput").ap()
    wv_d = nc.dram_tensor("wv", [D, D], F32, kind="ExternalInput").ap()
    wo_d = nc.dram_tensor("wo", [D, D], F32, kind="ExternalInput").ap()
    rw_d = nc.dram_tensor("rw", [D, NEXP], F32, kind="ExternalInput").ap()
    rb_d = nc.dram_tensor("rb", [NEXP, 1], F32, kind="ExternalInput").ap()
    w1_d = nc.dram_tensor("w1", [NEXP, D, DFF], F32, kind="ExternalInput").ap()
    b1_d = nc.dram_tensor("b1", [128, NEXP, 32], F32, kind="ExternalInput").ap()
    w2_d = nc.dram_tensor("w2", [NEXP, DFF, D], F32, kind="ExternalInput").ap()
    b2_d = nc.dram_tensor("b2", [NEXP, D], F32, kind="ExternalInput").ap()
    id8_d = nc.dram_tensor("id8", [8, 8], F32, kind="ExternalInput").ap()
    iota_d = nc.dram_tensor("iota16", [128, nb * 16], I16, kind="ExternalInput").ap()
    pos_d = nc.dram_tensor("pos16", [16, CAP // 16], F32, kind="ExternalInput").ap()
    esel_d = nc.dram_tensor("esel", [1, NEXP], F32, kind="ExternalInput").ap()

    y_d = nc.dram_tensor("y", [ntok + 128, D], F32, kind="ExternalOutput").ap()
    dbg = {}
    if debug:
        dbg["xnT"] = nc.dram_tensor("dbg_xnT", [128, 8, S], BF16, kind="ExternalOutput").ap()
        dbg["sc"] = nc.dram_tensor("dbg_sc", [128, nb * 16, 8], F32, kind="ExternalOutput").ap()
        dbg["tau"] = nc.dram_tensor("dbg_tau", [128, 8], F32, kind="ExternalOutput").ap()
        dbg["nf"] = nc.dram_tensor("dbg_nf", [1, NEXP], U32, kind="ExternalOutput").ap()
        if nb == 1:
            dbg["qt"] = nc.dram_tensor("dbg_qt", [128, S], BF16, kind="ExternalOutput").ap()
            dbg["kt"] = nc.dram_tensor("dbg_kt", [128, S], BF16, kind="ExternalOutput").ap()
            dbg["vaug"] = nc.dram_tensor("dbg_vaug", [128, 16, 80], BF16, kind="ExternalOutput").ap()
            dbg["attnT"] = nc.dram_tensor("dbg_attnT", [128, 8, S], BF16, kind="ExternalOutput").ap()
            dbg["lrow"] = nc.dram_tensor("dbg_lrow", [1, 4096], F32, kind="ExternalOutput").ap()
            dbg["ln"] = nc.dram_tensor("dbg_ln", [128, 8], F32, kind="ExternalOutput").ap()
            dbg["lnr16"] = nc.dram_tensor("dbg_lnr16", [128, 16], F32, kind="ExternalOutput").ap()
            dbg["lnm16"] = nc.dram_tensor("dbg_lnm16", [128, 16], F32, kind="ExternalOutput").ap()
            dbg["lnv16"] = nc.dram_tensor("dbg_lnv16", [128, 16], F32, kind="ExternalOutput").ap()
            dbg["xn0"] = nc.dram_tensor("dbg_xn0", [128, D], BF16, kind="ExternalOutput").ap()

    xt_tab = nc.dram_tensor("xt_tab", [ntok + 128, D], BF16)
    sct_tab = nc.dram_tensor("sct_tab", [ntok + 128, 64], F32)
    cc_in = nc.dram_tensor("cc_in", [128, nb * 16, 8], F32)
    cc_out = nc.dram_tensor("cc_out", [8, 128, nb * 16, 8], F32)
    cc_tin = nc.dram_tensor("cc_tin", [1, 1], F32)
    cc_tout = nc.dram_tensor("cc_tout", [8, 1], F32)

    y_rows = y_d[0:ntok, :].rearrange("(b p s) d -> b p s d", b=nb, p=128)
    xt_rows = xt_tab.ap()[0:ntok, :].rearrange("(b p s) d -> b p s d", b=nb, p=128)
    sct_rows = sct_tab.ap()[0:ntok, :].rearrange("(b p s) d -> b p s d", b=nb, p=128)

    ctx = ExitStack()
    with ctx:
        ARENA_BYTES = 206 * 1024
        arena = ctx.enter_context(nc.sbuf_tensor("arena", [128, ARENA_BYTES], U8))
        cursor = [0]

        def carve(nbytes, align=128):
            off = (cursor[0] + align - 1) // align * align
            cursor[0] = off + nbytes
            pass
            return off

        def view(off, shape, dt_):
            esz = {F32: 4, BF16: 2, I16: 2, U8: 1, U32: 4}[dt_]
            n = int(np.prod(shape[1:])) * esz
            ap = arena[0:shape[0], off:off + n].bitcast(dt_)
            if len(shape) == 2:
                return ap
            if len(shape) == 3:
                return ap.rearrange("p (a b) -> p a b", a=shape[1])
            if len(shape) == 4:
                return ap.rearrange("p (a b c) -> p a b c", a=shape[1], b=shape[2])
            raise ValueError

        def alloc(shape, dt_):
            esz = {F32: 4, BF16: 2, I16: 2, U8: 1, U32: 4}[dt_]
            off = carve(int(np.prod(shape[1:])) * esz)
            return view(off, shape, dt_), off

        iota_s, _ = alloc([128, nb * 16], I16)
        pos_s, _ = alloc([16, CAP // 16], F32)
        id8_s, _ = alloc([8, 8], F32)
        rb_s, _ = alloc([8, 1], F32)
        b1_s, _ = alloc([128, NEXP, 32], F32)
        scores_all, _ = alloc([128, nb * 16, 8], F32)
        idxs = []
        for e in range(NEXP):
            t, _ = alloc([128, CAP // 16], I16)
            idxs.append(t)
        nf_s, _ = alloc([1, NEXP], U32)
        kth_s, _ = alloc([1, 16], F32)
        tau_b, _ = alloc([128, 8], F32)
        esel_s, _ = alloc([1, NEXP], F32)
        esel_b, _ = alloc([128, NEXP], F32)
        tb1, _ = alloc([128, 1], F32)
        tau8, _ = alloc([1, 8], F32)
        ones64, _ = alloc([1, 64], F32)
        epst, _ = alloc([128, 1], F32)
        if debug:
            dbgln, _ = alloc([128, 8], F32)
            dbgr16, _ = alloc([128, 16], F32)
            dbgm16, _ = alloc([128, 16], F32)
            dbgv16, _ = alloc([128, 16], F32)

        base = cursor[0]

        wq_s, _ = alloc([128, 8, D], BF16)
        wk_s, _ = alloc([128, 8, D], BF16)
        wv_s, _ = alloc([128, 8, D], BF16)
        wo_s, _ = alloc([128, 8, D], BF16)
        rw_s, _ = alloc([128, 8, 8], BF16)
        xnT, o_xnT = alloc([128, 8, S], BF16)
        attnT, o_attnT = alloc([128, 8, S], BF16)
        qt, o_qt = alloc([128, S], BF16)
        kt_, _ = alloc([128, S], BF16)
        vt, _ = alloc([128, S], BF16)
        vaug0, _ = alloc([128, 16, 80], BF16)
        vaug1, _ = alloc([128, 16, 80], BF16)
        pt, _ = alloc([128, 2, 1024], BF16)
        xt3, _ = alloc([128, 2, D], F32)
        x1t, _ = alloc([128, 2, D], F32)
        xn2b, _ = alloc([128, 2, D], BF16)
        stg_odd, o_stg = alloc([64, 2, 1024], BF16)
        sqd = pt[:, 0, :]
        ssum, _ = alloc([128, 2], F32)
        sumsq, _ = alloc([128, 2], F32)
        lnm, _ = alloc([128, 2], F32)
        lnv, _ = alloc([128, 2], F32)
        lnr, _ = alloc([128, 2], F32)
        sub = cursor[0]
        Lq = arena[64:65, o_stg:o_stg + 4096 * 4].bitcast(F32)
        linv, _ = alloc([1, 4096], F32)
        Lp2, _ = alloc([128, 32], F32)
        l_end = cursor[0]
        cursor[0] = sub
        lg_s, _ = alloc([8, S], F32)
        se_t, _ = alloc([128, 2, 8], F32)
        sden, _ = alloc([128, 2], F32)
        cursor[0] = max(cursor[0], l_end)
        attn_end = cursor[0]

        cursor[0] = base
        scg, _ = alloc([128, 512, 8], F32)
        w1c, o_w1c = alloc([128, 2, 8, 1024], BF16)
        w2c, _ = alloc([128, 2, 8, 1024], BF16)
        h1T, _ = alloc([128, 32, CAP], BF16)
        gt_s, _ = alloc([128, NCH, 8, 128], BF16)
        gsc_s, _ = alloc([128, NCH, 64], F32)
        contrib, _ = alloc([128, NCH, D], F32)
        b2b, _ = alloc([128, D], F32)
        b2row, _ = alloc([1, D], F32)
        msk8, _ = alloc([128, nb * 16], U8)
        sel16, _ = alloc([128, nb * 16], I16)
        neg16, _ = alloc([128, nb * 16], I16)
        selR, _ = alloc([16, nb * 128], I16)
        nff, _ = alloc([1, 1], F32)
        nfb, _ = alloc([16, 1], F32)
        pmask, _ = alloc([16, CAP // 16], U8)
        idum, _ = alloc([16, CAP // 16], I16)
        my_sc = view(o_w1c, [128, 512], F32)
        negt = view(o_w1c + 2048, [128, 512], F32)
        m512 = view(o_w1c + 4096, [128, 512], U8)
        ffn_end = cursor[0]
        assert max(attn_end, ffn_end) <= ARENA_BYTES

        ps_a = ctx.enter_context(nc.psum_tensor("ps_a", [128, 2048], F32))
        ps_b = ctx.enter_context(nc.psum_tensor("ps_b", [128, 2048], F32))

        sem_names = ["d_g", "d_s", "d_sy", "d_st", "c_v", "c_s", "c_t", "c_g", "c_cc"]
        sems = {n: ctx.enter_context(nc.semaphore(n)) for n in sem_names}
        blk = ctx.enter_context(nc.Block())

        def program(eng, E):
            C = {n: 0 for n in sem_names}
            M = {}

            def me(name):
                return eng == name

            def w(sem, val):
                E.wait_ge(sems[sem], val)

            def inc(inst, sem, n=1):
                if inst is not None:
                    inst.then_inc(sems[sem], n)
                C[sem] += n
                return C[sem]

            def dma(q, out, in_, sem, transpose=False, slow=False):
                i = (E.dma_start(out=out, in_=in_, transpose=transpose,
                                 allow_slow_non_contiguous=slow) if me(q) else None)
                return inc(i, sem, 16)

            for (dst, src) in ((wq_s, wq_d), (wk_s, wk_d), (wv_s, wv_d), (wo_s, wo_d)):
                dma("gpsimd", dst[:], src.rearrange("(ft p) d -> p ft d", p=128), "d_g")
            dma("gpsimd", rw_s[:], rw_d.rearrange("(ft p) e -> p ft e", p=128), "d_g")
            t_wload = dma("gpsimd", b1_s[:], b1_d, "d_g")
            dma("scalar", rb_s[:], rb_d, "d_s")
            dma("scalar", id8_s[:], id8_d, "d_s")
            dma("scalar", iota_s[:], iota_d, "d_s")
            dma("scalar", pos_s[:], pos_d, "d_s")
            t_cload = dma("scalar", esel_s[:], esel_d, "d_s")
            if me("vector"):
                E.memset(ones64[:], 1.0)
                E.memset(epst, EPS)
                E.memset(vaug0[:, :, 64:65], 1.0)
                E.memset(vaug1[:, :, 64:65], 1.0)
                i = E.memset(sqd, 0.0)
            else:
                i = None
            t_init = inc(i, "c_v")
            if me("scalar"):
                w("c_v", t_init)
            dma("scalar", xt_tab.ap()[ntok:ntok + 128, :], sqd, "d_s")
            t_ztab = dma("scalar", sct_tab.ap()[ntok:ntok + 128, :], sqd.bitcast(F32)[:, 0:64], "d_s")
            M_zt = t_ztab

            def emit_ln(xsrc, lnslot, xn_dst):
                if me("vector"):
                    E.tensor_mul(sqd, xsrc, xsrc)
                    E.reduce_sum(sumsq[:, lnslot:lnslot + 1], sqd, axis=AX.X)
                    i = E.reduce_sum(ssum[:, lnslot:lnslot + 1], xsrc, axis=AX.X)
                else:
                    i = None
                t_red = inc(i, "c_v")
                if me("vector"):
                    w("c_v", t_red)
                    if ("lnv_free", lnslot) in M:
                        w("c_s", M[("lnv_free", lnslot)])
                    i = E.tensor_scalar_mul(lnm[:, lnslot:lnslot + 1], ssum[:, lnslot:lnslot + 1], 1.0 / D)
                else:
                    i = None
                tt1 = inc(i, "c_v")
                if me("vector"):
                    w("c_v", tt1)
                    i = E.tensor_mul(lnv[:, lnslot:lnslot + 1], lnm[:, lnslot:lnslot + 1], lnm[:, lnslot:lnslot + 1])
                else:
                    i = None
                tt2 = inc(i, "c_v")
                if me("vector"):
                    w("c_v", tt2)
                    i = E.scalar_tensor_tensor(lnv[:, lnslot:lnslot + 1], sumsq[:, lnslot:lnslot + 1],
                                               1.0 / D, lnv[:, lnslot:lnslot + 1], ALU.mult, ALU.subtract)
                else:
                    i = None
                t_stats = inc(i, "c_v")
                if me("scalar"):
                    w("c_v", t_stats)
                    E.activation(lnr[:, lnslot:lnslot + 1], lnv[:, lnslot:lnslot + 1], AF.Ln, bias=epst)
                    i = E.activation(lnr[:, lnslot:lnslot + 1], lnr[:, lnslot:lnslot + 1], AF.Exp, scale=-0.5)
                else:
                    i = None
                t_r = inc(i, "c_s")
                M[("lnv_free", lnslot)] = t_r
                if me("vector"):
                    w("c_s", t_r)
                    i = E.tensor_scalar(xn_dst, xsrc, lnm[:, lnslot:lnslot + 1],
                                        lnr[:, lnslot:lnslot + 1], ALU.subtract, ALU.mult)
                else:
                    i = None
                return inc(i, "c_v")

            def emit_division(hp):
                t_all_ev = M[("pair_ev", hp)]
                if me("sync"):
                    w("c_v", t_all_ev)
                tok = dma("sync", Lp2[:], Lq[0:1, :], "d_sy")
                if me("vector"):
                    w("d_sy", tok)
                    i = E.reciprocal(Lp2[:], Lp2[:])
                else:
                    i = None
                t_rc = inc(i, "c_v")
                if me("sync"):
                    w("c_v", t_rc)
                t_linv = dma("sync", linv[:], Lp2[:], "d_sy")
                for u in range(4):
                    hh, qc = u // 2, u % 2
                    for chn in range(2):
                        lbslot = (u * 2 + chn) % 2
                        if me("tensor"):
                            w("d_sy", t_linv)
                            if ("lb", lbslot) in M:
                                w("c_v", M[("lb", lbslot)])
                            i = E.matmul(ps_b[0:64, lbslot * 512:(lbslot + 1) * 512],
                                         ones64[:],
                                         linv[0:1, u * 1024 + chn * 512: u * 1024 + (chn + 1) * 512],
                                         start=True, stop=True)
                        else:
                            i = None
                        t_lb = inc(i, "c_t")
                        if me("vector"):
                            w("c_t", t_lb)
                            if hh == 0:
                                sl = attnT[0:64, hp, qc * 1024 + chn * 512: qc * 1024 + (chn + 1) * 512]
                            else:
                                sl = stg_odd[:, qc, chn * 512:(chn + 1) * 512]
                            i = E.tensor_mul(sl, sl, ps_b[0:64, lbslot * 512:(lbslot + 1) * 512])
                        else:
                            i = None
                        t_div = inc(i, "c_v")
                        M[("lb", lbslot)] = t_div
                M[("ot_ev", 0)] = max(M.get(("ot_ev", 0), 0), t_div)
                M[("ot_ev", 1)] = max(M.get(("ot_ev", 1), 0), t_div)
                if me("sync"):
                    w("c_v", t_div)
                for qc in range(2):
                    dma("sync", attnT[64:128, hp, qc * 1024:(qc + 1) * 1024], stg_odd[:, qc, :], "d_sy")
                M[("div_done", hp)] = t_div
                M["stg_free"] = C["d_sy"]

            for b in range(nb):
                for s in range(NSL):
                    xslot, lnslot = s % 2, s % 2
                    if me("scalar") and ("x_free", xslot) in M:
                        w("c_v", M[("x_free", xslot)])
                    tok = dma("scalar", xt3[:, xslot, :], x_in[b, s * 128:(s + 1) * 128, :], "d_s")
                    if me("vector"):
                        w("d_s", max(tok, M_zt))
                        if ("xn_free", lnslot) in M:
                            w("d_sy", M[("xn_free", lnslot)])
                    t_xn = emit_ln(xt3[:, xslot, :], lnslot, xn2b[:, lnslot, :])
                    M[("x_free", xslot)] = t_xn
                    if debug and nb == 1 and b == 0:
                        if me("vector"):
                            E.tensor_copy(dbgr16[:, s:s + 1], lnr[:, lnslot:lnslot + 1])
                            E.tensor_copy(dbgm16[:, s:s + 1], lnm[:, lnslot:lnslot + 1])
                            i = E.tensor_copy(dbgv16[:, s:s + 1], lnv[:, lnslot:lnslot + 1])
                        else:
                            i = None
                        inc(i, "c_v")
                    if debug and nb == 1 and b == 0 and s == 0:
                        if me("vector"):
                            w("c_v", t_xn)
                            E.tensor_copy(dbgln[:, 0:1], ssum[:, 0:1])
                            E.tensor_copy(dbgln[:, 1:2], sumsq[:, 0:1])
                            E.tensor_copy(dbgln[:, 2:3], lnm[:, 0:1])
                            E.tensor_copy(dbgln[:, 3:4], lnv[:, 0:1])
                            i = E.tensor_copy(dbgln[:, 4:5], lnr[:, 0:1])
                        else:
                            i = None
                        tdv = inc(i, "c_v")
                        if me("sync"):
                            w("c_v", tdv)
                        dma("sync", dbg["ln"], dbgln[:], "d_st")
                        dma("sync", dbg["xn0"], xn2b[:, 0, :], "d_st")
                    if me("sync"):
                        w("c_v", t_xn)
                    for ft in range(8):
                        dma("sync", xnT[:, ft, s * 128:(s + 1) * 128],
                            xn2b[:, lnslot, ft * 128:(ft + 1) * 128], "d_sy", transpose=True)
                    M[("xn_free", lnslot)] = C["d_sy"]
                t_xnT = C["d_sy"]
                if debug and b == 0 and nb == 1:
                    if me("sync"):
                        w("d_sy", t_xnT)
                        w("c_v", C["c_v"])
                    dma("sync", dbg["xnT"], xnT[:], "d_st")
                    dma("sync", dbg["lnr16"], dbgr16[:], "d_st")
                    dma("sync", dbg["lnm16"], dbgm16[:], "d_st")
                    dma("sync", dbg["lnv16"], dbgv16[:], "d_st")

                for hp in range(8):
                    for mi, (wmat, dst) in enumerate(((wq_s, qt), (wk_s, kt_), (wv_s, vt))):
                        if me("tensor"):
                            w("d_sy", t_xnT)
                            w("d_g", t_wload)
                            cvw = max(M.get("qkv_free", 0), M.get(("proj_free", 0), 0),
                                      M.get(("proj_free", 1), 0))
                            if cvw:
                                w("c_v", cvw)
                            csw = max(M.get("st_free", 0), M.get("router_ps_free", 0))
                            if csw:
                                w("c_s", csw)
                        i = None
                        for ft in range(8):
                            for ch in range(4):
                                if me("tensor"):
                                    i = E.matmul(ps_a[:, ch * 512:(ch + 1) * 512],
                                                 wmat[:, ft, hp * 128:(hp + 1) * 128],
                                                 xnT[:, ft, ch * 512:(ch + 1) * 512],
                                                 start=(ft == 0), stop=(ft == 7))
                        t_mm = inc(i, "c_t")
                        if me("vector"):
                            w("c_t", t_mm)
                            i = E.tensor_copy(dst[:], ps_a[:])
                        else:
                            i = None
                        M["qkv_free"] = inc(i, "c_v")
                    t_qkv = M["qkv_free"]
                    if hp > 0:
                        emit_division(hp - 1)
                    if me("sync"):
                        w("c_v", t_qkv)
                    for ktile in range(16):
                        dma("sync", vaug0[:, ktile, 0:64], vt[0:64, ktile * 128:(ktile + 1) * 128], "d_sy", transpose=True)
                        dma("sync", vaug1[:, ktile, 0:64], vt[64:128, ktile * 128:(ktile + 1) * 128], "d_sy", transpose=True)
                    t_vaug = C["d_sy"]

                    if debug and nb == 1 and b == 0 and hp == 0:
                        if me("sync"):
                            w("d_sy", t_vaug)
                        dma("sync", dbg["qt"], qt[:], "d_st")
                        dma("sync", dbg["kt"], kt_[:], "d_st")
                        dma("sync", dbg["vaug"], vaug0[:].rearrange("p a b -> p (a b)").rearrange("p (a b) -> p a b", a=16)[:, :, 0:80], "d_st")
                    for hh in range(2):
                        hbase = 64 * hh
                        for qc in range(2):
                            u = hh * 2 + qc
                            otp = u % 2
                            for ktile in range(16):
                                stp = ktile % 2
                                if me("tensor"):
                                    if ktile < 2:
                                        w("d_sy", t_vaug)
                                        if ("exp", stp) in M:
                                            w("c_s", M[("exp", stp)])
                                    else:
                                        w("c_s", M[("exp", stp)])
                                i = None
                                for n in range(2):
                                    if me("tensor"):
                                        i = E.matmul(ps_a[:, stp * 1024 + n * 512: stp * 1024 + (n + 1) * 512],
                                                     kt_[hbase:hbase + 64, ktile * 128:(ktile + 1) * 128],
                                                     qt[hbase:hbase + 64, qc * 1024 + n * 512: qc * 1024 + (n + 1) * 512],
                                                     start=True, stop=True)
                                t_st = inc(i, "c_t")
                                if me("scalar"):
                                    w("c_t", t_st)
                                    if ("pt_free", stp) in M:
                                        w("c_t", M[("pt_free", stp)])
                                    i = E.activation(pt[:, stp, :], ps_a[:, stp * 1024:(stp + 1) * 1024],
                                                     AF.Exp, scale=SCALE)
                                else:
                                    i = None
                                t_exp = inc(i, "c_s")
                                M[("exp", stp)] = t_exp
                                if ktile >= 14:
                                    M["st_free"] = t_exp
                                i = None
                                for n in range(2):
                                    if me("tensor"):
                                        w("c_s", t_exp)
                                        if ktile == 0 and n == 0 and ("ot_ev", otp) in M:
                                            w("c_v", M[("ot_ev", otp)])
                                        i = E.matmul(ps_b[0:65, otp * 1024 + n * 512: otp * 1024 + (n + 1) * 512],
                                                     (vaug0 if hh == 0 else vaug1)[:, ktile, 0:65],
                                                     pt[:, stp, n * 512:(n + 1) * 512],
                                                     start=(ktile == 0), stop=(ktile == 15))
                                t_pv = inc(i, "c_t")
                                M[("pt_free", stp)] = t_pv
                            if me("vector"):
                                w("c_t", t_pv)
                                if hh == 1 and "stg_free" in M:
                                    w("d_sy", M["stg_free"])
                                E.tensor_copy(Lq[0:1, u * 1024:(u + 1) * 1024],
                                              ps_b[64:65, otp * 1024:(otp + 1) * 1024])
                                if hh == 0:
                                    i = E.tensor_copy(attnT[0:64, hp, qc * 1024:(qc + 1) * 1024],
                                                      ps_b[0:64, otp * 1024:(otp + 1) * 1024])
                                else:
                                    i = E.tensor_copy(stg_odd[:, qc, :],
                                                      ps_b[0:64, otp * 1024:(otp + 1) * 1024])
                            else:
                                i = None
                            M[("ot_ev", otp)] = inc(i, "c_v")
                    M[("pair_ev", hp)] = C["c_v"]
                emit_division(7)
                if debug and nb == 1 and b == 0:
                    tok = C["d_sy"]
                    if me("sync"):
                        w("d_sy", tok)
                    dma("sync", dbg["attnT"], attnT[:], "d_st")
                    dma("sync", dbg["lrow"], Lq[0:1, :], "d_st")

                t_attnT = C["d_sy"]
                for s in range(NSL):
                    xslot, lnslot = s % 2, s % 2
                    if me("scalar") and ("x_free", xslot) in M:
                        w("c_v", M[("x_free", xslot)])
                    tok_x = dma("scalar", xt3[:, xslot, :], x_in[b, s * 128:(s + 1) * 128, :], "d_s")
                    if me("tensor"):
                        w("d_sy", t_attnT)
                        if ("proj_free", s % 2) in M:
                            w("c_v", M[("proj_free", s % 2)])
                    i = None
                    for ft in range(8):
                        for ch in range(2):
                            if me("tensor"):
                                i = E.matmul(ps_a[:, (s % 2) * 1024 + ch * 512:(s % 2) * 1024 + (ch + 1) * 512],
                                             attnT[:, ft, s * 128:(s + 1) * 128],
                                             wo_s[:, ft, ch * 512:(ch + 1) * 512],
                                             start=(ft == 0), stop=(ft == 7))
                    t_proj = inc(i, "c_t")
                    if me("vector"):
                        w("c_t", t_proj)
                        w("d_s", tok_x)
                        if ("x1_free", s % 2) in M:
                            w("d_st", M[("x1_free", s % 2)])
                        i = E.tensor_add(x1t[:, s % 2, :], xt3[:, xslot, :],
                                         ps_a[:, (s % 2) * 1024:(s % 2) * 1024 + 1024])
                    else:
                        i = None
                    t_x1 = inc(i, "c_v")
                    M[("proj_free", s % 2)] = t_x1
                    if me("scalar"):
                        w("c_v", t_x1)
                    tok_y = dma("scalar", y_rows[b, :, s, :], x1t[:, s % 2, :], "d_st")
                    if me("vector"):
                        if ("xn_free", lnslot) in M:
                            w("d_sy", M[("xn_free", lnslot)])
                        if ("xn2_free", lnslot) in M:
                            w("d_st", M[("xn2_free", lnslot)])
                    t_xn = emit_ln(x1t[:, s % 2, :], lnslot, xn2b[:, lnslot, :])
                    M[("x_free", xslot)] = t_xn
                    M[("x1_free", s % 2)] = tok_y
                    if me("scalar"):
                        w("c_v", t_xn)
                    dma("scalar", xt_rows[b, :, s, :], xn2b[:, lnslot, :], "d_st")
                    M[("xn2_free", lnslot)] = C["d_st"]
                    if me("sync"):
                        w("c_v", t_xn)
                    for ft in range(8):
                        dma("sync", xnT[:, ft, s * 128:(s + 1) * 128],
                            xn2b[:, lnslot, ft * 128:(ft + 1) * 128], "d_sy", transpose=True)
                    M[("xn_free", lnslot)] = C["d_sy"]

                t_xnT2 = C["d_sy"]
                if me("tensor"):
                    w("d_sy", t_xnT2)
                    w("c_v", M[("div_done", 7)])
                    if "lg_free" in M:
                        w("c_v", M["lg_free"])
                i = None
                for ft in range(8):
                    for ch in range(4):
                        if me("tensor"):
                            i = E.matmul(ps_b[0:8, ch * 512:(ch + 1) * 512],
                                         rw_s[:, ft, 0:8],
                                         xnT[:, ft, ch * 512:(ch + 1) * 512],
                                         start=(ft == 0), stop=(ft == 7))
                t_lg = inc(i, "c_t")
                if me("vector"):
                    w("c_t", t_lg)
                    w("d_s", t_cload)
                    i = E.tensor_scalar(lg_s[:], ps_b[0:8, :], rb_s[0:8, 0:1], None, ALU.add)
                else:
                    i = None
                t_lgs = inc(i, "c_v")
                M["lg_free"] = t_lgs
                for s in range(NSL):
                    if me("tensor"):
                        w("c_v", t_lgs)
                        if ("se_free", s % 2) in M:
                            w("c_s", M[("se_free", s % 2)])
                        if ("sc_free", s % 2) in M:
                            w("c_v", M[("sc_free", s % 2)])
                        i = E.transpose(ps_a[:, (s % 2) * 512:(s % 2) * 512 + 8],
                                        lg_s[0:8, s * 128:(s + 1) * 128], id8_s[:])
                    else:
                        i = None
                    t_tp = inc(i, "c_t")
                    if me("scalar"):
                        w("c_t", t_tp)
                        if ("sc_free", s % 2) in M:
                            w("c_v", M[("sc_free", s % 2)])
                        i = E.activation(se_t[:, s % 2, :], ps_a[:, (s % 2) * 512:(s % 2) * 512 + 8],
                                         AF.Exp, accum_out=sden[:, s % 2:s % 2 + 1])
                    else:
                        i = None
                    t_se = inc(i, "c_s")
                    M[("se_free", s % 2)] = t_se
                    if me("vector"):
                        w("c_s", t_se)
                        i = E.reciprocal(sden[:, s % 2:s % 2 + 1], sden[:, s % 2:s % 2 + 1])
                    else:
                        i = None
                    t_rd = inc(i, "c_v")
                    if me("vector"):
                        w("c_v", t_rd)
                        i = E.tensor_scalar_mul(scores_all[:, b * 16 + s, :], se_t[:, s % 2, :],
                                                sden[:, s % 2:s % 2 + 1])
                    else:
                        i = None
                    t_sc = inc(i, "c_v")
                    M[("sc_free", s % 2)] = t_sc
                    M["router_ps_free"] = t_se
                t_scb = C["c_v"]
                M[("ot_ev", 0)] = max(M.get(("ot_ev", 0), 0), t_lgs)
                M[("ot_ev", 1)] = max(M.get(("ot_ev", 1), 0), t_lgs)
                if me("scalar"):
                    w("c_v", t_scb)
                dma("scalar", sct_rows[b][:, :, 0:8], scores_all[:, b * 16:(b + 1) * 16, :], "d_st")
                dma("scalar", cc_in.ap()[:, b * 16:(b + 1) * 16, :],
                    scores_all[:, b * 16:(b + 1) * 16, :], "d_st")

            if me("vector"):
                w("d_st", C["d_st"])
                i = E.memset(x1t[:, 0, :], 0.0)
            else:
                i = None
            tz = inc(i, "c_v")
            if me("scalar"):
                w("c_v", tz)
            dma("scalar", y_d[ntok:ntok + 128, :], x1t[:, 0, :], "d_st")

            if debug:
                tok = C["c_v"]
                if me("scalar"):
                    w("c_v", tok)
                dma("scalar", dbg["sc"], scores_all[:], "d_st")

            if not do_ffn:
                if me("sync"):
                    for sname in ("d_st", "d_s", "d_g", "d_sy"):
                        w(sname, C[sname])
                return

            t_ccin = C["d_st"]
            if me("gpsimd"):
                w("d_st", t_ccin)
                i = E.collective_compute("AllGather", ALU.bypass,
                                         replica_groups=[list(range(8))],
                                         ins=[cc_in.ap().opt()], outs=[cc_out.ap().opt()])
            else:
                i = None
            t_cc = inc(i, "c_cc")
            if me("scalar"):
                w("c_cc", t_cc)
            for r in range(8):
                dma("scalar", scg[:, r * (nb * 16):(r + 1) * (nb * 16), :], cc_out.ap()[r], "d_s")
            t_scg = C["d_s"]

            if me("gpsimd"):
                w("d_s", t_cload)
                i = E.partition_broadcast(esel_b[:], esel_s[0:1, :], 128)
            else:
                i = None
            t_eb = inc(i, "c_g")
            if me("vector"):
                w("c_g", t_eb)
                w("d_s", t_scg)
                E.memset(negt[:], -1e30)
                i = E.tensor_scalar_mul(my_sc[:], scg[:, :, 0:1].rearrange("p n o -> p (n o)"),
                                        esel_b[:, 0:1])
            else:
                i = None
            tok = inc(i, "c_v")
            for e in range(1, NEXP):
                if me("vector"):
                    i = E.scalar_tensor_tensor(my_sc[:], scg[:, :, e:e + 1].rearrange("p n o -> p (n o)"),
                                               esel_b[:, e:e + 1], my_sc[:], ALU.mult, ALU.add)
                else:
                    i = None
                tok = inc(i, "c_v")
            t_mysc = tok
            for rnd, kk in enumerate((510, 510, 510, 510, 8)):
                if me("gpsimd"):
                    w("c_v", t_mysc)
                    i = E.kth_largest(kth_s[0:1, 0:2], my_sc[:], 512, kk, quantile=1e-6)
                else:
                    i = None
                t_k = inc(i, "c_g")
                if rnd < 4:
                    if me("gpsimd"):
                        w("c_g", t_k)
                        i = E.partition_broadcast(tb1[:], kth_s[0:1, 0:1], 128)
                    else:
                        i = None
                    t_tb = inc(i, "c_g")
                    if me("vector"):
                        w("c_g", t_tb)
                        E.tensor_scalar(m512[:], my_sc[:], tb1[:, 0:1], None, ALU.is_gt)
                        i = E.select(my_sc[:], m512[:], negt[:], my_sc[:])
                    else:
                        i = None
                    t_mysc = inc(i, "c_v")
            if me("scalar"):
                w("c_g", t_k)
            tok = dma("scalar", cc_tin.ap(), kth_s[0:1, 0:1], "d_s")
            if me("gpsimd"):
                w("d_s", tok)
                i = E.collective_compute("AllGather", ALU.bypass,
                                         replica_groups=[list(range(8))],
                                         ins=[cc_tin.ap().opt()], outs=[cc_tout.ap().opt()])
            else:
                i = None
            t_cc2 = inc(i, "c_cc")
            if me("scalar"):
                w("c_cc", t_cc2)
            tok = dma("scalar", tau8[:], cc_tout.ap().rearrange("e o -> o e"), "d_s")
            if me("gpsimd"):
                w("d_s", tok)
                i = E.partition_broadcast(tau_b[:], tau8[0:1, :], 128)
            else:
                i = None
            t_tau = inc(i, "c_g")
            if debug:
                if me("scalar"):
                    w("c_g", t_tau)
                dma("scalar", dbg["tau"], tau_b[:], "d_st")

            for e in range(NEXP):
                if me("vector"):
                    w("c_g", t_tau)
                    if e == 0:
                        E.memset(neg16[:], -1)
                        E.memset(idum[:], NT)
                    if "selR_free" in M:
                        w("c_g", M["selR_free"])
                    E.tensor_scalar(msk8[:], scores_all[:, :, e:e + 1].rearrange("p n o -> p (n o)"),
                                    tau_b[:, e:e + 1], None, ALU.is_gt)
                    E.memset(idxs[e][0:16, :], NT)
                    i = E.select(sel16[:], msk8[:], iota_s[:], neg16[:])
                else:
                    i = None
                t_sel = inc(i, "c_v")
                if me("sync"):
                    w("c_v", t_sel)
                for k in range(8):
                    dma("sync", selR[:, nb * 16 * k: nb * 16 * (k + 1)],
                        sel16[16 * k:16 * (k + 1), :], "d_sy")
                t_selR = C["d_sy"]
                if me("gpsimd"):
                    w("d_sy", t_selR)
                    i = E.sparse_gather(idxs[e][0:16, :], selR[:], num_found=nf_s[0:1, e:e + 1])
                else:
                    i = None
                t_sg = inc(i, "c_g")
                M["selR_free"] = t_sg
                if me("vector"):
                    w("c_g", t_sg)
                    i = E.tensor_copy(nff[:], nf_s[0:1, e:e + 1])
                else:
                    i = None
                t_nff = inc(i, "c_v")
                if me("gpsimd"):
                    w("c_v", t_nff)
                    i = E.partition_broadcast(nfb[:], nff[0:1, :], 16)
                else:
                    i = None
                t_nfb = inc(i, "c_g")
                if me("vector"):
                    w("c_g", t_nfb)
                    E.tensor_scalar(pmask[:], pos_s[:], nfb[:, 0:1], None, ALU.is_ge)
                    i = E.select(idxs[e][0:16, :], pmask[:], idum[:], idxs[e][0:16, :])
                else:
                    i = None
                t_fix = inc(i, "c_v")
                if me("sync"):
                    w("c_v", t_fix)
                tk = dma("sync", idxs[e][16:32, :], idxs[e][0:16, :], "d_sy")
                if me("sync"):
                    w("d_sy", tk)
                tk = dma("sync", idxs[e][32:64, :], idxs[e][0:32, :], "d_sy")
                if me("sync"):
                    w("d_sy", tk)
                tk = dma("sync", idxs[e][64:128, :], idxs[e][0:64, :], "d_sy")
                M[("idx_done", e)] = tk
            if debug:
                tok = C["c_g"]
                if me("scalar"):
                    w("c_g", tok)
                dma("scalar", dbg["nf"], nf_s[:], "d_st")

            t_y_x1 = C["d_st"]
            for e in range(NEXP):
                if me("gpsimd"):
                    w("d_sy", M[("idx_done", e)])
                    if "gt_free" in M:
                        w("c_v", M["gt_free"])
                for k in range(NCH):
                    i = (E.dma_gather(gt_s[:, k], xt_tab.ap(),
                                      idxs[e][:, 8 * k:8 * (k + 1)], 128, 128, 1024, transpose=True)
                         if me("gpsimd") else None)
                    inc(i, "d_g", 16)
                    i = (E.dma_gather(gsc_s[:, k:k + 1, :].bitcast(BF16), sct_tab.ap().bitcast(BF16),
                                      idxs[e][:, 8 * k:8 * (k + 1)], 128, 128, 128)
                         if me("gpsimd") else None)
                    inc(i, "d_g", 16)
                t_gath = C["d_g"]
                if me("scalar") and "b2_free" in M:
                    w("c_g", M["b2_free"])
                tok_b2 = dma("scalar", b2row[:], b2_d[e:e + 1, :], "d_s")
                if me("gpsimd"):
                    w("d_s", tok_b2)
                    if "b2b_free" in M:
                        w("c_v", M["b2b_free"])
                    i = E.partition_broadcast(b2b[:], b2row[0:1, :], 128)
                else:
                    i = None
                t_b2b = inc(i, "c_g")
                M["b2_free"] = t_b2b
                for dffc in range(4):
                    wslot = dffc % 2
                    if me("gpsimd") and ("w1_free", wslot) in M:
                        w("c_t", M[("w1_free", wslot)])
                    tok_w1 = dma("gpsimd", w1c[:, wslot, :, :],
                                 w1_d[e].rearrange("(ft p) f -> p ft f", p=128)[:, :, dffc * 1024:(dffc + 1) * 1024],
                                 "d_g")
                    for dt in range(8):
                        pslot = dt % 2
                        if me("tensor"):
                            if dffc == 0 and dt == 0:
                                w("d_g", t_gath)
                            w("d_g", tok_w1)
                            if ("h1_free", pslot) in M:
                                w("c_v", M[("h1_free", pslot)])
                        i = None
                        for ft in range(8):
                            if me("tensor"):
                                i = E.matmul(ps_a[:, pslot * 512:pslot * 512 + CAP],
                                             w1c[:, wslot, ft, dt * 128:(dt + 1) * 128],
                                             gt_s[:, :, ft, :],
                                             start=(ft == 0), stop=(ft == 7))
                        t_h1 = inc(i, "c_t")
                        M[("w1_free", wslot)] = t_h1
                        if me("vector"):
                            w("c_t", t_h1)
                            i = E.tensor_scalar(h1T[:, dffc * 8 + dt, :], ps_a[:, pslot * 512:pslot * 512 + CAP],
                                                b1_s[:, e, dffc * 8 + dt:dffc * 8 + dt + 1], 0.0,
                                                ALU.add, ALU.max)
                        else:
                            i = None
                        M[("h1_free", pslot)] = inc(i, "c_v")
                t_h1all = C["c_v"]
                eo_slice = [ps_a[:, 0:1024], ps_a[:, 1024:2048], ps_b[:, 0:1024]]
                for dffc in range(4):
                    wslot = dffc % 2
                    if me("gpsimd") and ("w2_free", wslot) in M:
                        w("c_t", M[("w2_free", wslot)])
                    tok_w2 = dma("gpsimd", w2c[:, wslot, :, :],
                                 w2_d[e].rearrange("(c ft p) f -> c p ft f", p=128, ft=8)[dffc],
                                 "d_g")
                    i = None
                    for dt in range(8):
                        d32 = dffc * 8 + dt
                        for ts in range(NCH):
                            for ch in range(2):
                                if me("tensor"):
                                    if dffc == 0 and dt == 0 and ts == 0 and ch == 0:
                                        w("c_v", t_h1all)
                                        if "eo_free" in M:
                                            w("c_v", M["eo_free"])
                                    w("d_g", tok_w2)
                                    i = E.matmul(eo_slice[ts][:, ch * 512:(ch + 1) * 512],
                                                 h1T[:, d32, ts * 128:(ts + 1) * 128],
                                                 w2c[:, wslot, dt, ch * 512:(ch + 1) * 512],
                                                 start=(d32 == 0), stop=(d32 == 31))
                    t_eo = inc(i, "c_t")
                    M[("w2_free", wslot)] = t_eo
                for ts in range(NCH):
                    if me("vector"):
                        w("c_t", t_eo)
                        w("c_g", t_b2b)
                        w("d_g", t_gath)
                        if ("ctr_free", ts) in M:
                            w("d_g", M[("ctr_free", ts)])
                        E.tensor_add(contrib[:, ts, :], b2b[:], eo_slice[ts])
                        i = E.tensor_scalar_mul(contrib[:, ts, :], contrib[:, ts, :], gsc_s[:, ts, e:e + 1])
                    else:
                        i = None
                    inc(i, "c_v")
                t_ctr = C["c_v"]
                M["eo_free"] = t_ctr
                M["gt_free"] = t_ctr
                M["b2b_free"] = t_ctr
                if me("gpsimd"):
                    w("c_v", t_ctr)
                    if e == 0:
                        w("d_st", t_y_x1)
                    else:
                        w("d_g", M["scat_done"])
                for k in range(NCH):
                    i = (E.dma_scatter_add(y_d, contrib[:, k:k + 1, :],
                                           idxs[e][:, 8 * k:8 * (k + 1)], 128, 128, 1024)
                         if me("gpsimd") else None)
                    tk = inc(i, "d_g", 16)
                    M[("ctr_free", k)] = tk
                M["scat_done"] = C["d_g"]

            if me("sync"):
                for sname in ("d_st", "d_s", "d_g", "d_sy"):
                    w(sname, C[sname])

        for eng, starter in (("sync", blk.sync), ("scalar", blk.scalar), ("vector", blk.vector),
                             ("tensor", blk.tensor), ("gpsimd", blk.gpsimd)):
            starter(lambda E, eng=eng: program(eng, E))
    nc.finalize()
    return nc


def make_inputs(inp, core):
    iota = np.zeros((128, 64), np.int16)
    for p in range(128):
        for c in range(64):
            iota[p, c] = (c // 16) * 2048 + p * 16 + (c % 16)
    pos = (np.arange(16)[:, None] + np.arange(CAP // 16)[None, :] * 16).astype(np.float32)
    inp = {k: np.asarray(v) for k, v in inp.items()}
    return {
        "x": np.ascontiguousarray(inp["L0_attn_x"][core * NB:(core + 1) * NB], dtype=np.float32),
        "wq": inp["L0_attn_Wq"], "wk": inp["L0_attn_Wk"],
        "wv": inp["L0_attn_Wv"], "wo": inp["L0_attn_Wo"],
        "rw": inp["L0_ffn_router_w"], "rb": np.ascontiguousarray(inp["L0_ffn_router_b"].reshape(8, 1)),
        "w1": inp["L0_ffn_w1"],
        "b1": np.ascontiguousarray(np.asarray(inp["L0_ffn_b1"]).reshape(8, 32, 128).transpose(2, 0, 1)),
        "w2": inp["L0_ffn_w2"], "b2": inp["L0_ffn_b2"],
        "id8": np.eye(8, dtype=np.float32),
        "iota16": iota, "pos16": pos,
        "esel": np.eye(8, dtype=np.float32)[core:core + 1],
    }


def unshard_y(y):
    return y[:NT].reshape(NB, 128, 16, D).transpose(0, 2, 1, 3).reshape(NB, S, D)


_NC_CACHE = {}


def _get_nc():
    if "nc" not in _NC_CACHE:
        _NC_CACHE["nc"] = build(debug=False, do_ffn=True, nb=NB)
    return _NC_CACHE["nc"]


def kernel(**inputs):
    from concourse.bass_utils import run_bass_kernel_spmd
    nc = _get_nc()
    in_maps = [make_inputs(inputs, c) for c in range(8)]
    res = run_bass_kernel_spmd(nc, in_maps, core_ids=list(range(8)))
    y = np.concatenate([unshard_y(res.results[c]["y"]) for c in range(8)], 0)
    return y


# version 03
# speedup vs baseline: 1.6275x; 1.6275x over previous
import numpy as np
import concourse.bacc as bacc
import concourse.bass as bass
import concourse.mybir as mybir
import concourse.hw_specs as hw_specs
from contextlib import ExitStack

_orig_get_act_tables = hw_specs.get_activation_tables
_AFT = mybir.ActivationFunctionType


def _patched_act_tables(arch):
    out = {}
    for name, funcs in _orig_get_act_tables(arch).items():
        f = set(funcs)
        if name != "natural_log_exp_and_others":
            f.discard(_AFT.Exp)
            f.discard(_AFT.Ln)
        out[name] = f
    return out


hw_specs.get_activation_tables = _patched_act_tables

F32 = mybir.dt.float32
BF16 = mybir.dt.bfloat16
I16 = mybir.dt.int16
U8 = mybir.dt.uint8
U32 = mybir.dt.uint32
ALU = mybir.AluOpType
AF = mybir.ActivationFunctionType
AX = mybir.AxisListType

NB, S, D, H = 4, 2048, 1024, 16
NT = NB * S
NSL = S // 128
CAP = 384
NCH = CAP // 128
DFF = 4096
NEXP = 8
EPS = 1e-5
SCALE = 0.125


def build(debug=False, do_ffn=True, nb=NB):
    nc = bacc.Bacc("TRN2")
    ntok = nb * S
    x_in = nc.dram_tensor("x", [nb, S, D], F32, kind="ExternalInput").ap()
    wq_d = nc.dram_tensor("wq", [D, D], F32, kind="ExternalInput").ap()
    wk_d = nc.dram_tensor("wk", [D, D], F32, kind="ExternalInput").ap()
    wv_d = nc.dram_tensor("wv", [D, D], F32, kind="ExternalInput").ap()
    wo_d = nc.dram_tensor("wo", [D, D], F32, kind="ExternalInput").ap()
    rw_d = nc.dram_tensor("rw", [D, NEXP], F32, kind="ExternalInput").ap()
    rb_d = nc.dram_tensor("rb", [NEXP, 1], F32, kind="ExternalInput").ap()
    w1_d = nc.dram_tensor("w1", [NEXP, D, DFF], F32, kind="ExternalInput").ap()
    b1_d = nc.dram_tensor("b1", [128, NEXP, 32], F32, kind="ExternalInput").ap()
    w2_d = nc.dram_tensor("w2", [NEXP, DFF, D], F32, kind="ExternalInput").ap()
    b2_d = nc.dram_tensor("b2", [NEXP, D], F32, kind="ExternalInput").ap()
    id8_d = nc.dram_tensor("id8", [8, 8], F32, kind="ExternalInput").ap()
    iota_d = nc.dram_tensor("iota16", [128, nb * 16], I16, kind="ExternalInput").ap()
    pos_d = nc.dram_tensor("pos16", [16, CAP // 16], F32, kind="ExternalInput").ap()
    esel_d = nc.dram_tensor("esel", [1, NEXP], F32, kind="ExternalInput").ap()

    y_d = nc.dram_tensor("y", [ntok + 128, D], F32, kind="ExternalOutput").ap()
    dbg = {}
    if debug:
        dbg["xnT"] = nc.dram_tensor("dbg_xnT", [128, 8, S], BF16, kind="ExternalOutput").ap()
        dbg["sc"] = nc.dram_tensor("dbg_sc", [128, nb * 16, 8], F32, kind="ExternalOutput").ap()
        dbg["tau"] = nc.dram_tensor("dbg_tau", [128, 8], F32, kind="ExternalOutput").ap()
        dbg["nf"] = nc.dram_tensor("dbg_nf", [1, NEXP], U32, kind="ExternalOutput").ap()
        if nb == 1:
            dbg["qt"] = nc.dram_tensor("dbg_qt", [128, S], BF16, kind="ExternalOutput").ap()
            dbg["kt"] = nc.dram_tensor("dbg_kt", [128, S], BF16, kind="ExternalOutput").ap()
            dbg["vaug"] = nc.dram_tensor("dbg_vaug", [128, 16, 80], BF16, kind="ExternalOutput").ap()
            dbg["attnT"] = nc.dram_tensor("dbg_attnT", [128, 8, S], BF16, kind="ExternalOutput").ap()
            dbg["lrow"] = nc.dram_tensor("dbg_lrow", [1, 4096], F32, kind="ExternalOutput").ap()
            dbg["ln"] = nc.dram_tensor("dbg_ln", [128, 8], F32, kind="ExternalOutput").ap()
            dbg["lnr16"] = nc.dram_tensor("dbg_lnr16", [128, 16], F32, kind="ExternalOutput").ap()
            dbg["lnm16"] = nc.dram_tensor("dbg_lnm16", [128, 16], F32, kind="ExternalOutput").ap()
            dbg["lnv16"] = nc.dram_tensor("dbg_lnv16", [128, 16], F32, kind="ExternalOutput").ap()
            dbg["xn0"] = nc.dram_tensor("dbg_xn0", [128, D], BF16, kind="ExternalOutput").ap()

    xt_tab = nc.dram_tensor("xt_tab", [ntok + 128, D], BF16)
    sct_tab = nc.dram_tensor("sct_tab", [ntok + 128, 64], F32)
    cc_in = nc.dram_tensor("cc_in", [128, nb * 16, 8], F32)
    cc_out = nc.dram_tensor("cc_out", [8, 128, nb * 16, 8], F32)
    cc_tin = nc.dram_tensor("cc_tin", [1, 1], F32)
    cc_tout = nc.dram_tensor("cc_tout", [8, 1], F32)

    y_rows = y_d[0:ntok, :].rearrange("(b p s) d -> b p s d", b=nb, p=128)
    xt_rows = xt_tab.ap()[0:ntok, :].rearrange("(b p s) d -> b p s d", b=nb, p=128)
    sct_rows = sct_tab.ap()[0:ntok, :].rearrange("(b p s) d -> b p s d", b=nb, p=128)

    ctx = ExitStack()
    with ctx:
        ARENA_BYTES = 206 * 1024
        arena = ctx.enter_context(nc.sbuf_tensor("arena", [128, ARENA_BYTES], U8))
        cursor = [0]

        def carve(nbytes, align=128):
            off = (cursor[0] + align - 1) // align * align
            cursor[0] = off + nbytes
            pass
            return off

        def view(off, shape, dt_):
            esz = {F32: 4, BF16: 2, I16: 2, U8: 1, U32: 4}[dt_]
            n = int(np.prod(shape[1:])) * esz
            ap = arena[0:shape[0], off:off + n].bitcast(dt_)
            if len(shape) == 2:
                return ap
            if len(shape) == 3:
                return ap.rearrange("p (a b) -> p a b", a=shape[1])
            if len(shape) == 4:
                return ap.rearrange("p (a b c) -> p a b c", a=shape[1], b=shape[2])
            raise ValueError

        import os as _os
        def alloc(shape, dt_, _dbg=[0]):
            esz = {F32: 4, BF16: 2, I16: 2, U8: 1, U32: 4}[dt_]
            off = carve(int(np.prod(shape[1:])) * esz)
            if _os.environ.get("ARENA_DBG"):
                print("alloc", shape, dt_, "->", off, cursor[0])
            return view(off, shape, dt_), off

        iota_s, _ = alloc([128, nb * 16], I16)
        pos_s, _ = alloc([16, CAP // 16], F32)
        id8_s, _ = alloc([8, 8], F32)
        rb_s, _ = alloc([8, 1], F32)
        b1_s, _ = alloc([128, NEXP, 32], F32)
        scores_all, _ = alloc([128, nb * 16, 8], F32)
        idxs = []
        for e in range(NEXP):
            t, _ = alloc([128, CAP // 16], I16)
            idxs.append(t)
        nf_s, _ = alloc([1, NEXP], U32)
        kth_s, _ = alloc([1, 16], F32)
        tau_b, _ = alloc([128, 8], F32)
        esel_s, _ = alloc([1, NEXP], F32)
        esel_b, _ = alloc([128, NEXP], F32)
        tau8, _ = alloc([1, 8], F32)
        ones64, _ = alloc([1, 64], F32)
        epst, _ = alloc([128, 1], F32)
        if debug:
            dbgln, _ = alloc([128, 8], F32)
            dbgr16, _ = alloc([128, 16], F32)
            dbgm16, _ = alloc([128, 16], F32)
            dbgv16, _ = alloc([128, 16], F32)

        base = cursor[0]

        wq_s, _ = alloc([128, 8, D], BF16)
        wk_s, _ = alloc([128, 8, D], BF16)
        wv_s, _ = alloc([128, 8, D], BF16)
        wo_s, _ = alloc([128, 8, D], BF16)
        rw_s, _ = alloc([128, 8, 8], BF16)
        xnT, o_xnT = alloc([128, 8, S], BF16)
        attnT, o_attnT = alloc([128, 8, S], BF16)
        qt, o_qt = alloc([128, S], BF16)
        kt_, _ = alloc([128, S], BF16)
        vt, _ = alloc([128, S], BF16)
        vaug0, _ = alloc([128, 16, 80], BF16)
        vaug1, _ = alloc([128, 16, 80], BF16)
        pt, _ = alloc([128, 2, 1024], BF16)
        xt3, _ = alloc([128, 3, D], F32)
        x1t, _ = alloc([128, 2, D], F32)
        xn2b, _ = alloc([128, 2, D], BF16)
        stg_odd, o_stg = alloc([64, 2, 1024], BF16)
        sqd = pt[:, 0, :]
        ssum, _ = alloc([128, 2], F32)
        sumsq, _ = alloc([128, 2], F32)
        lnm, _ = alloc([128, 2], F32)
        lnv, _ = alloc([128, 2], F32)
        lnr, _ = alloc([128, 2], F32)
        sub = cursor[0]
        Lq = arena[64:65, o_stg:o_stg + 4096 * 4].bitcast(F32)
        linv, _ = alloc([1, 4096], F32)
        Lp2, _ = alloc([128, 32], F32)
        lb_sb, _ = alloc([64, 1024], F32)
        l_end = cursor[0]
        cursor[0] = sub
        lg_s, _ = alloc([8, S], F32)
        se_t, _ = alloc([128, 2, 8], F32)
        sden, _ = alloc([128, 2], F32)
        cursor[0] = max(cursor[0], l_end)
        attn_end = cursor[0]

        cursor[0] = base
        scg, _ = alloc([128, 512, 8], F32)
        w1c, o_w1c = alloc([128, 2, 8, 1024], BF16)
        w2c, _ = alloc([128, 2, 8, 1024], BF16)
        h1T, _ = alloc([128, 32, CAP], BF16)
        gt_s, _ = alloc([128, NCH, 8, 128], BF16)
        gsc_s, _ = alloc([128, NCH, 64], F32)
        contrib, _ = alloc([128, NCH, D], F32)
        b2b, _ = alloc([128, D], F32)
        b2row, _ = alloc([1, D], F32)
        msk8, _ = alloc([128, nb * 16], U8)
        sel16, _ = alloc([128, nb * 16], I16)
        neg16, _ = alloc([128, nb * 16], I16)
        selR, _ = alloc([16, nb * 128], I16)
        nff, _ = alloc([1, 1], F32)
        nfb, _ = alloc([16, 1], F32)
        pmask, _ = alloc([16, CAP // 16], U8)
        idum, _ = alloc([16, CAP // 16], I16)
        my_sc = view(o_w1c, [128, 512], F32)
        mskf = view(o_w1c + 2048, [128, 512], F32)
        bis = view(o_w1c + 4096, [128, 8], F32)
        cntp = view(o_w1c + 4160, [128, 1], F32)
        onem = view(o_w1c + 4224, [128, 128], F32)
        ffn_end = cursor[0]
        print(f"arena: attn={attn_end} ffn={ffn_end} cap={ARENA_BYTES}")
        assert max(attn_end, ffn_end) <= ARENA_BYTES

        ps_a = ctx.enter_context(nc.psum_tensor("ps_a", [128, 2048], F32))
        ps_b = ctx.enter_context(nc.psum_tensor("ps_b", [128, 2048], F32))

        sem_names = ["d_g", "d_s", "d_sy", "d_st", "c_v", "c_s", "c_t", "c_g", "c_cc"]
        sems = {n: ctx.enter_context(nc.semaphore(n)) for n in sem_names}
        blk = ctx.enter_context(nc.Block())

        def program(eng, E):
            C = {n: 0 for n in sem_names}
            M = {}

            def me(name):
                return eng == name

            def w(sem, val):
                E.wait_ge(sems[sem], val)

            def inc(inst, sem, n=1):
                if inst is not None:
                    inst.then_inc(sems[sem], n)
                C[sem] += n
                return C[sem]

            def dma(q, out, in_, sem, transpose=False, slow=False):
                i = (E.dma_start(out=out, in_=in_, transpose=transpose,
                                 allow_slow_non_contiguous=slow) if me(q) else None)
                return inc(i, sem, 16)

            for (dst, src) in ((wq_s, wq_d), (wk_s, wk_d), (wv_s, wv_d), (wo_s, wo_d)):
                dma("gpsimd", dst[:], src.rearrange("(ft p) d -> p ft d", p=128), "d_g")
            dma("gpsimd", rw_s[:], rw_d.rearrange("(ft p) e -> p ft e", p=128), "d_g")
            t_wload = dma("gpsimd", b1_s[:], b1_d, "d_g")
            dma("scalar", rb_s[:], rb_d, "d_s")
            dma("scalar", id8_s[:], id8_d, "d_s")
            dma("scalar", iota_s[:], iota_d, "d_s")
            dma("scalar", pos_s[:], pos_d, "d_s")
            t_cload = dma("scalar", esel_s[:], esel_d, "d_s")
            if me("vector"):
                E.memset(ones64[:], 1.0)
                E.memset(epst, EPS)
                E.memset(vaug0[:, :, 64:65], 1.0)
                E.memset(vaug1[:, :, 64:65], 1.0)
                i = E.memset(sqd, 0.0)
            else:
                i = None
            t_init = inc(i, "c_v")
            if me("scalar"):
                w("c_v", t_init)
            dma("scalar", xt_tab.ap()[ntok:ntok + 128, :], sqd, "d_s")
            t_ztab = dma("scalar", sct_tab.ap()[ntok:ntok + 128, :], sqd.bitcast(F32)[:, 0:64], "d_s")
            M_zt = t_ztab

            def emit_ln_stats(xsrc, lnslot):
                if me("vector"):
                    E.tensor_mul(sqd, xsrc, xsrc)
                    E.reduce_sum(sumsq[:, lnslot:lnslot + 1], sqd, axis=AX.X)
                    i = E.reduce_sum(ssum[:, lnslot:lnslot + 1], xsrc, axis=AX.X)
                else:
                    i = None
                t_red = inc(i, "c_v")
                if me("vector"):
                    w("c_v", t_red)
                    if ("lnv_free", lnslot) in M:
                        w("c_s", M[("lnv_free", lnslot)])
                    i = E.tensor_scalar_mul(lnm[:, lnslot:lnslot + 1], ssum[:, lnslot:lnslot + 1], 1.0 / D)
                else:
                    i = None
                tt1 = inc(i, "c_v")
                if me("vector"):
                    w("c_v", tt1)
                    i = E.tensor_mul(lnv[:, lnslot:lnslot + 1], lnm[:, lnslot:lnslot + 1], lnm[:, lnslot:lnslot + 1])
                else:
                    i = None
                tt2 = inc(i, "c_v")
                if me("vector"):
                    w("c_v", tt2)
                    i = E.scalar_tensor_tensor(lnv[:, lnslot:lnslot + 1], sumsq[:, lnslot:lnslot + 1],
                                               1.0 / D, lnv[:, lnslot:lnslot + 1], ALU.mult, ALU.subtract)
                else:
                    i = None
                t_stats = inc(i, "c_v")
                if me("scalar"):
                    w("c_v", t_stats)
                    E.activation(lnr[:, lnslot:lnslot + 1], lnv[:, lnslot:lnslot + 1], AF.Ln, bias=epst)
                    i = E.activation(lnr[:, lnslot:lnslot + 1], lnr[:, lnslot:lnslot + 1], AF.Exp, scale=-0.5)
                else:
                    i = None
                t_r = inc(i, "c_s")
                M[("lnv_free", lnslot)] = t_r
                return t_r

            def emit_ln_apply(xsrc, lnslot, xn_dst, t_r):
                if me("vector"):
                    w("c_s", t_r)
                    i = E.tensor_scalar(xn_dst, xsrc, lnm[:, lnslot:lnslot + 1],
                                        lnr[:, lnslot:lnslot + 1], ALU.subtract, ALU.mult)
                else:
                    i = None
                return inc(i, "c_v")

            def emit_ln(xsrc, lnslot, xn_dst):
                return emit_ln_apply(xsrc, lnslot, xn_dst, emit_ln_stats(xsrc, lnslot))

            def emit_division(hp):
                t_all_ev = M[("pair_ev", hp)]
                if me("sync"):
                    w("c_v", t_all_ev)
                tok = dma("sync", Lp2[:], Lq[0:1, :], "d_sy")
                if me("vector"):
                    w("d_sy", tok)
                    i = E.reciprocal(Lp2[:], Lp2[:])
                else:
                    i = None
                t_rc = inc(i, "c_v")
                if me("sync"):
                    w("c_v", t_rc)
                t_linv = dma("sync", linv[:], Lp2[:], "d_sy")
                for u in range(4):
                    hh, qc = u // 2, u % 2
                    if me("gpsimd"):
                        w("d_sy", t_linv)
                        if "lbg" in M:
                            w("c_v", M["lbg"])
                        i = E.partition_broadcast(lb_sb[:],
                                                  linv[0:1, u * 1024:(u + 1) * 1024], 64)
                    else:
                        i = None
                    t_lb = inc(i, "c_g")
                    if me("vector"):
                        w("c_g", t_lb)
                        if hh == 0:
                            sl = attnT[0:64, hp, qc * 1024:(qc + 1) * 1024]
                        else:
                            sl = stg_odd[:, qc, :]
                        i = E.tensor_mul(sl, sl, lb_sb[:])
                    else:
                        i = None
                    t_div = inc(i, "c_v")
                    M["lbg"] = t_div
                M[("ot_ev", 0)] = max(M.get(("ot_ev", 0), 0), t_div)
                M[("ot_ev", 1)] = max(M.get(("ot_ev", 1), 0), t_div)
                if me("sync"):
                    w("c_v", t_div)
                for qc in range(2):
                    dma("sync", attnT[64:128, hp, qc * 1024:(qc + 1) * 1024], stg_odd[:, qc, :], "d_sy")
                M[("div_done", hp)] = t_div
                M["stg_free"] = C["d_sy"]

            for b in range(nb):
                ln_tr = {}
                for s in range(NSL + 1):
                    if s < NSL:
                        xslot, lnslot = s % 3, s % 2
                        if me("scalar") and ("x_free", xslot) in M:
                            w("c_v", M[("x_free", xslot)])
                        tok = dma("scalar", xt3[:, xslot, :], x_in[b, s * 128:(s + 1) * 128, :], "d_s")
                        if me("vector"):
                            w("d_s", max(tok, M_zt))
                        ln_tr[s] = emit_ln_stats(xt3[:, xslot, :], lnslot)
                    if s >= 1:
                        sp = s - 1
                        xslot, lnslot = sp % 3, sp % 2
                        if me("vector") and ("xn_free", lnslot) in M:
                            w("d_sy", M[("xn_free", lnslot)])
                        t_xn = emit_ln_apply(xt3[:, xslot, :], lnslot, xn2b[:, lnslot, :], ln_tr[sp])
                        M[("x_free", xslot)] = t_xn
                        if me("sync"):
                            w("c_v", t_xn)
                        for ft in range(8):
                            dma("sync", xnT[:, ft, sp * 128:(sp + 1) * 128],
                                xn2b[:, lnslot, ft * 128:(ft + 1) * 128], "d_sy", transpose=True)
                        M[("xn_free", lnslot)] = C["d_sy"]
                t_xnT = C["d_sy"]
                if debug and b == 0 and nb == 1:
                    if me("sync"):
                        w("d_sy", t_xnT)
                        w("c_v", C["c_v"])
                    dma("sync", dbg["xnT"], xnT[:], "d_st")
                    dma("sync", dbg["lnr16"], dbgr16[:], "d_st")
                    dma("sync", dbg["lnm16"], dbgm16[:], "d_st")
                    dma("sync", dbg["lnv16"], dbgv16[:], "d_st")

                for hp in range(8):
                    for mi, (wmat, dst) in enumerate(((wv_s, vt), (wq_s, qt), (wk_s, kt_))):
                        if me("tensor"):
                            w("d_sy", t_xnT)
                            w("d_g", t_wload)
                            cvw = max(M.get(("proj_free", 0), 0), M.get(("proj_free", 1), 0))
                            if cvw:
                                w("c_v", cvw)
                            csw = max(M.get("st_free", 0), M.get("router_ps_free", 0))
                            if csw:
                                w("c_s", csw)
                        t_ch = {}
                        i = None
                        for ft in range(8):
                            for ch in range(4):
                                if me("tensor"):
                                    if ft == 0 and ("qkv_ev_ch", ch) in M:
                                        w("c_v", M[("qkv_ev_ch", ch)])
                                    i = E.matmul(ps_a[:, ch * 512:(ch + 1) * 512],
                                                 wmat[:, ft, hp * 128:(hp + 1) * 128],
                                                 xnT[:, ft, ch * 512:(ch + 1) * 512],
                                                 start=(ft == 0), stop=(ft == 7))
                                if ft == 7:
                                    t_ch[ch] = inc(i, "c_t")
                        for ch in range(4):
                            if me("vector"):
                                w("c_t", t_ch[ch])
                                i = E.tensor_copy(dst[:, ch * 512:(ch + 1) * 512], ps_a[:, ch * 512:(ch + 1) * 512])
                            else:
                                i = None
                            M[("qkv_ev_ch", ch)] = inc(i, "c_v")
                        M["qkv_free"] = M[("qkv_ev_ch", 3)]
                        if mi == 0:
                            if me("sync"):
                                w("c_v", M["qkv_free"])
                            t_vaug_kt = {}
                            for ktile in range(16):
                                dma("sync", vaug0[:, ktile, 0:64], vt[0:64, ktile * 128:(ktile + 1) * 128], "d_sy", transpose=True)
                                t_vaug_kt[ktile] = dma("sync", vaug1[:, ktile, 0:64], vt[64:128, ktile * 128:(ktile + 1) * 128], "d_sy", transpose=True)
                    t_qkv = M["qkv_free"]
                    t_vaug = t_vaug_kt[15]
                    if hp > 0:
                        emit_division(hp - 1)

                    if debug and nb == 1 and b == 0 and hp == 0:
                        if me("sync"):
                            w("d_sy", t_vaug)
                        dma("sync", dbg["qt"], qt[:], "d_st")
                        dma("sync", dbg["kt"], kt_[:], "d_st")
                        dma("sync", dbg["vaug"], vaug0[:].rearrange("p a b -> p (a b)").rearrange("p (a b) -> p a b", a=16)[:, :, 0:80], "d_st")
                    for hh in range(2):
                        hbase = 64 * hh
                        for qc in range(2):
                            u = hh * 2 + qc
                            otp = u % 2
                            t_exp_of = {}
                            for step in range(17):
                                if step < 16:
                                    ktile = step
                                    stp = ktile % 2
                                    if me("tensor"):
                                        if ktile < 2:
                                            w("c_v", t_qkv)
                                        if ("exp", stp) in M:
                                            w("c_s", M[("exp", stp)])
                                    i = None
                                    for n in range(2):
                                        if me("tensor"):
                                            i = E.matmul(ps_a[:, stp * 1024 + n * 512: stp * 1024 + (n + 1) * 512],
                                                         kt_[hbase:hbase + 64, ktile * 128:(ktile + 1) * 128],
                                                         qt[hbase:hbase + 64, qc * 1024 + n * 512: qc * 1024 + (n + 1) * 512],
                                                         start=True, stop=True)
                                    t_st = inc(i, "c_t")
                                    if me("scalar"):
                                        w("c_t", t_st)
                                        if ("pt_free", stp) in M:
                                            w("c_t", M[("pt_free", stp)])
                                        i = E.activation(pt[:, stp, :], ps_a[:, stp * 1024:(stp + 1) * 1024],
                                                         AF.Exp, scale=SCALE)
                                    else:
                                        i = None
                                    t_exp = inc(i, "c_s")
                                    M[("exp", stp)] = t_exp
                                    t_exp_of[ktile] = t_exp
                                    if ktile >= 14:
                                        M["st_free"] = t_exp
                                if step >= 1:
                                    ktile = step - 1
                                    stp = ktile % 2
                                    i = None
                                    for n in range(2):
                                        if me("tensor"):
                                            w("c_s", t_exp_of[ktile])
                                            if n == 0:
                                                w("d_sy", t_vaug_kt[ktile])
                                            if ktile == 0 and n == 0 and ("ot_ev", otp) in M:
                                                w("c_v", M[("ot_ev", otp)])
                                            i = E.matmul(ps_b[0:65, otp * 1024 + n * 512: otp * 1024 + (n + 1) * 512],
                                                         (vaug0 if hh == 0 else vaug1)[:, ktile, 0:65],
                                                         pt[:, stp, n * 512:(n + 1) * 512],
                                                         start=(ktile == 0), stop=(ktile == 15))
                                    t_pv = inc(i, "c_t")
                                    M[("pt_free", stp)] = t_pv
                            if me("vector"):
                                w("c_t", t_pv)
                                if hh == 1 and "stg_free" in M:
                                    w("d_sy", M["stg_free"])
                                E.tensor_copy(Lq[0:1, u * 1024:(u + 1) * 1024],
                                              ps_b[64:65, otp * 1024:(otp + 1) * 1024])
                                if hh == 0:
                                    i = E.tensor_copy(attnT[0:64, hp, qc * 1024:(qc + 1) * 1024],
                                                      ps_b[0:64, otp * 1024:(otp + 1) * 1024])
                                else:
                                    i = E.tensor_copy(stg_odd[:, qc, :],
                                                      ps_b[0:64, otp * 1024:(otp + 1) * 1024])
                            else:
                                i = None
                            M[("ot_ev", otp)] = inc(i, "c_v")
                    M[("pair_ev", hp)] = C["c_v"]
                emit_division(7)
                if debug and nb == 1 and b == 0:
                    tok = C["d_sy"]
                    if me("sync"):
                        w("d_sy", tok)
                    dma("sync", dbg["attnT"], attnT[:], "d_st")
                    dma("sync", dbg["lrow"], Lq[0:1, :], "d_st")

                t_attnT = C["d_sy"]
                for s in range(NSL):
                    xslot, lnslot = s % 2, s % 2
                    if me("scalar") and ("x_free", xslot) in M:
                        w("c_v", M[("x_free", xslot)])
                    tok_x = dma("scalar", xt3[:, xslot, :], x_in[b, s * 128:(s + 1) * 128, :], "d_s")
                    if me("tensor"):
                        w("d_sy", t_attnT)
                        if ("proj_free", s % 2) in M:
                            w("c_v", M[("proj_free", s % 2)])
                    i = None
                    for ft in range(8):
                        for ch in range(2):
                            if me("tensor"):
                                i = E.matmul(ps_a[:, (s % 2) * 1024 + ch * 512:(s % 2) * 1024 + (ch + 1) * 512],
                                             attnT[:, ft, s * 128:(s + 1) * 128],
                                             wo_s[:, ft, ch * 512:(ch + 1) * 512],
                                             start=(ft == 0), stop=(ft == 7))
                    t_proj = inc(i, "c_t")
                    if me("vector"):
                        w("c_t", t_proj)
                        w("d_s", tok_x)
                        if ("x1_free", s % 2) in M:
                            w("d_st", M[("x1_free", s % 2)])
                        i = E.tensor_add(x1t[:, s % 2, :], xt3[:, xslot, :],
                                         ps_a[:, (s % 2) * 1024:(s % 2) * 1024 + 1024])
                    else:
                        i = None
                    t_x1 = inc(i, "c_v")
                    M[("proj_free", s % 2)] = t_x1
                    if me("scalar"):
                        w("c_v", t_x1)
                    tok_y = dma("scalar", y_rows[b, :, s, :], x1t[:, s % 2, :], "d_st")
                    if me("vector"):
                        if ("xn_free", lnslot) in M:
                            w("d_sy", M[("xn_free", lnslot)])
                        if ("xn2_free", lnslot) in M:
                            w("d_st", M[("xn2_free", lnslot)])
                    t_xn = emit_ln(x1t[:, s % 2, :], lnslot, xn2b[:, lnslot, :])
                    M[("x_free", xslot)] = t_xn
                    M[("x1_free", s % 2)] = tok_y
                    if me("scalar"):
                        w("c_v", t_xn)
                    dma("scalar", xt_rows[b, :, s, :], xn2b[:, lnslot, :], "d_st")
                    M[("xn2_free", lnslot)] = C["d_st"]
                    if me("sync"):
                        w("c_v", t_xn)
                    for ft in range(8):
                        dma("sync", xnT[:, ft, s * 128:(s + 1) * 128],
                            xn2b[:, lnslot, ft * 128:(ft + 1) * 128], "d_sy", transpose=True)
                    M[("xn_free", lnslot)] = C["d_sy"]

                t_xnT2 = C["d_sy"]
                if me("tensor"):
                    w("d_sy", t_xnT2)
                    w("c_v", M[("div_done", 7)])
                    if "lg_free" in M:
                        w("c_v", M["lg_free"])
                i = None
                for ft in range(8):
                    for ch in range(4):
                        if me("tensor"):
                            i = E.matmul(ps_b[0:8, ch * 512:(ch + 1) * 512],
                                         rw_s[:, ft, 0:8],
                                         xnT[:, ft, ch * 512:(ch + 1) * 512],
                                         start=(ft == 0), stop=(ft == 7))
                t_lg = inc(i, "c_t")
                if me("vector"):
                    w("c_t", t_lg)
                    w("d_s", t_cload)
                    i = E.tensor_scalar(lg_s[:], ps_b[0:8, :], rb_s[0:8, 0:1], None, ALU.add)
                else:
                    i = None
                t_lgs = inc(i, "c_v")
                M["lg_free"] = t_lgs
                for s in range(NSL):
                    if me("tensor"):
                        w("c_v", t_lgs)
                        if ("se_free", s % 2) in M:
                            w("c_s", M[("se_free", s % 2)])
                        if ("sc_free", s % 2) in M:
                            w("c_v", M[("sc_free", s % 2)])
                        i = E.transpose(ps_a[:, (s % 2) * 512:(s % 2) * 512 + 8],
                                        lg_s[0:8, s * 128:(s + 1) * 128], id8_s[:])
                    else:
                        i = None
                    t_tp = inc(i, "c_t")
                    if me("scalar"):
                        w("c_t", t_tp)
                        if ("sc_free", s % 2) in M:
                            w("c_v", M[("sc_free", s % 2)])
                        i = E.activation(se_t[:, s % 2, :], ps_a[:, (s % 2) * 512:(s % 2) * 512 + 8],
                                         AF.Exp, accum_out=sden[:, s % 2:s % 2 + 1])
                    else:
                        i = None
                    t_se = inc(i, "c_s")
                    M[("se_free", s % 2)] = t_se
                    if me("vector"):
                        w("c_s", t_se)
                        i = E.reciprocal(sden[:, s % 2:s % 2 + 1], sden[:, s % 2:s % 2 + 1])
                    else:
                        i = None
                    t_rd = inc(i, "c_v")
                    if me("vector"):
                        w("c_v", t_rd)
                        i = E.tensor_scalar_mul(scores_all[:, b * 16 + s, :], se_t[:, s % 2, :],
                                                sden[:, s % 2:s % 2 + 1])
                    else:
                        i = None
                    t_sc = inc(i, "c_v")
                    M[("sc_free", s % 2)] = t_sc
                    M["router_ps_free"] = t_se
                t_scb = C["c_v"]
                M[("ot_ev", 0)] = max(M.get(("ot_ev", 0), 0), t_lgs)
                M[("ot_ev", 1)] = max(M.get(("ot_ev", 1), 0), t_lgs)
                if me("scalar"):
                    w("c_v", t_scb)
                dma("scalar", sct_rows[b][:, :, 0:8], scores_all[:, b * 16:(b + 1) * 16, :], "d_st")
                dma("scalar", cc_in.ap()[:, b * 16:(b + 1) * 16, :],
                    scores_all[:, b * 16:(b + 1) * 16, :], "d_st")

            if me("vector"):
                w("d_st", C["d_st"])
                i = E.memset(x1t[:, 0, :], 0.0)
            else:
                i = None
            tz = inc(i, "c_v")
            if me("scalar"):
                w("c_v", tz)
            dma("scalar", y_d[ntok:ntok + 128, :], x1t[:, 0, :], "d_st")

            if debug:
                tok = C["c_v"]
                if me("scalar"):
                    w("c_v", tok)
                dma("scalar", dbg["sc"], scores_all[:], "d_st")

            if not do_ffn:
                if me("sync"):
                    for sname in ("d_st", "d_s", "d_g", "d_sy"):
                        w(sname, C[sname])
                return

            t_ccin = C["d_st"]
            if me("gpsimd"):
                w("d_st", t_ccin)
                i = E.collective_compute("AllGather", ALU.bypass,
                                         replica_groups=[list(range(8))],
                                         ins=[cc_in.ap().opt()], outs=[cc_out.ap().opt()])
            else:
                i = None
            t_cc = inc(i, "c_cc")
            if me("scalar"):
                w("c_cc", t_cc)
            for r in range(8):
                dma("scalar", scg[:, r * (nb * 16):(r + 1) * (nb * 16), :], cc_out.ap()[r], "d_s")
            t_scg = C["d_s"]

            if me("gpsimd"):
                w("d_s", t_cload)
                i = E.partition_broadcast(esel_b[:], esel_s[0:1, :], 128)
            else:
                i = None
            t_eb = inc(i, "c_g")
            if me("vector"):
                w("c_g", t_eb)
                w("d_s", t_scg)
                i = E.tensor_scalar_mul(my_sc[:], scg[:, :, 0:1].rearrange("p n o -> p (n o)"),
                                        esel_b[:, 0:1])
            else:
                i = None
            tok = inc(i, "c_v")
            for e in range(1, NEXP):
                if me("vector"):
                    i = E.scalar_tensor_tensor(my_sc[:], scg[:, :, e:e + 1].rearrange("p n o -> p (n o)"),
                                               esel_b[:, e:e + 1], my_sc[:], ALU.mult, ALU.add)
                else:
                    i = None
                tok = inc(i, "c_v")
            t_mysc = tok
            if me("vector"):
                w("c_v", t_mysc)
                E.memset(bis[:, 0:1], 0.0)
                E.memset(bis[:, 1:2], 1.0)
                E.memset(bis[:, 2:3], 0.5)
                i = E.memset(onem[:], 1.0)
            else:
                i = None
            t_bis = inc(i, "c_v")
            for it in range(36):
                if me("vector"):
                    w("c_v", t_bis)
                    E.tensor_scalar(mskf[:], my_sc[:], bis[:, 2:3], None, ALU.is_gt)
                    i = E.reduce_sum(cntp[:], mskf[:], axis=AX.X)
                else:
                    i = None
                t_cnt = inc(i, "c_v")
                if me("tensor"):
                    w("c_v", t_cnt)
                    i = E.matmul(ps_b[0:128, (it % 2) * 512:(it % 2) * 512 + 1],
                                 onem[:], cntp[:], start=True, stop=True)
                else:
                    i = None
                t_tot = inc(i, "c_t")
                if me("vector"):
                    w("c_t", t_tot)
                    i = E.tensor_scalar(bis[:, 4:5], ps_b[0:128, (it % 2) * 512:(it % 2) * 512 + 1],
                                        2048.0, None, ALU.is_ge)
                else:
                    i = None
                tb3 = inc(i, "c_v")
                if me("vector"):
                    w("c_v", tb3)
                    E.tensor_sub(bis[:, 5:6], bis[:, 2:3], bis[:, 0:1])
                    E.tensor_sub(bis[:, 6:7], bis[:, 1:2], bis[:, 2:3])
                    i = E.tensor_scalar(bis[:, 7:8], bis[:, 4:5], -1.0, 1.0, ALU.mult, ALU.add)
                else:
                    i = None
                tb4 = inc(i, "c_v")
                if me("vector"):
                    w("c_v", tb4)
                    E.tensor_mul(bis[:, 5:6], bis[:, 5:6], bis[:, 4:5])
                    i = E.tensor_mul(bis[:, 6:7], bis[:, 6:7], bis[:, 7:8])
                else:
                    i = None
                tb5 = inc(i, "c_v")
                if me("vector"):
                    w("c_v", tb5)
                    E.tensor_add(bis[:, 0:1], bis[:, 0:1], bis[:, 5:6])
                    i = E.tensor_sub(bis[:, 1:2], bis[:, 1:2], bis[:, 6:7])
                else:
                    i = None
                tb6 = inc(i, "c_v")
                if me("vector"):
                    w("c_v", tb6)
                    i = E.tensor_add(bis[:, 3:4], bis[:, 0:1], bis[:, 1:2])
                else:
                    i = None
                tb7 = inc(i, "c_v")
                if me("vector"):
                    w("c_v", tb7)
                    i = E.tensor_scalar_mul(bis[:, 2:3], bis[:, 3:4], 0.5)
                else:
                    i = None
                t_bis = inc(i, "c_v")
            if me("scalar"):
                w("c_v", t_bis)
            tok = dma("scalar", cc_tin.ap(), bis[0:1, 0:1], "d_s")
            if me("gpsimd"):
                w("d_s", tok)
                i = E.collective_compute("AllGather", ALU.bypass,
                                         replica_groups=[list(range(8))],
                                         ins=[cc_tin.ap().opt()], outs=[cc_tout.ap().opt()])
            else:
                i = None
            t_cc2 = inc(i, "c_cc")
            if me("scalar"):
                w("c_cc", t_cc2)
            tok = dma("scalar", tau8[:], cc_tout.ap().rearrange("e o -> o e"), "d_s")
            if me("gpsimd"):
                w("d_s", tok)
                i = E.partition_broadcast(tau_b[:], tau8[0:1, :], 128)
            else:
                i = None
            t_tau = inc(i, "c_g")
            if debug:
                if me("scalar"):
                    w("c_g", t_tau)
                dma("scalar", dbg["tau"], tau_b[:], "d_st")

            for e in range(NEXP):
                if me("vector"):
                    w("c_g", t_tau)
                    if e == 0:
                        E.memset(neg16[:], -1)
                        E.memset(idum[:], NT)
                    if "selR_free" in M:
                        w("c_g", M["selR_free"])
                    E.tensor_scalar(msk8[:], scores_all[:, :, e:e + 1].rearrange("p n o -> p (n o)"),
                                    tau_b[:, e:e + 1], None, ALU.is_gt)
                    E.memset(idxs[e][0:16, :], NT)
                    i = E.select(sel16[:], msk8[:], iota_s[:], neg16[:])
                else:
                    i = None
                t_sel = inc(i, "c_v")
                if me("sync"):
                    w("c_v", t_sel)
                for k in range(8):
                    dma("sync", selR[:, nb * 16 * k: nb * 16 * (k + 1)],
                        sel16[16 * k:16 * (k + 1), :], "d_sy")
                t_selR = C["d_sy"]
                if me("gpsimd"):
                    w("d_sy", t_selR)
                    i = E.sparse_gather(idxs[e][0:16, :], selR[:], num_found=nf_s[0:1, e:e + 1])
                else:
                    i = None
                t_sg = inc(i, "c_g")
                M["selR_free"] = t_sg
                if me("vector"):
                    w("c_g", t_sg)
                    i = E.tensor_copy(nff[:], nf_s[0:1, e:e + 1])
                else:
                    i = None
                t_nff = inc(i, "c_v")
                if me("gpsimd"):
                    w("c_v", t_nff)
                    i = E.partition_broadcast(nfb[:], nff[0:1, :], 16)
                else:
                    i = None
                t_nfb = inc(i, "c_g")
                if me("vector"):
                    w("c_g", t_nfb)
                    E.tensor_scalar(pmask[:], pos_s[:], nfb[:, 0:1], None, ALU.is_ge)
                    i = E.select(idxs[e][0:16, :], pmask[:], idum[:], idxs[e][0:16, :])
                else:
                    i = None
                t_fix = inc(i, "c_v")
                if me("sync"):
                    w("c_v", t_fix)
                tk = dma("sync", idxs[e][16:32, :], idxs[e][0:16, :], "d_sy")
                if me("sync"):
                    w("d_sy", tk)
                tk = dma("sync", idxs[e][32:64, :], idxs[e][0:32, :], "d_sy")
                if me("sync"):
                    w("d_sy", tk)
                tk = dma("sync", idxs[e][64:128, :], idxs[e][0:64, :], "d_sy")
                M[("idx_done", e)] = tk
            if debug:
                tok = C["c_g"]
                if me("scalar"):
                    w("c_g", tok)
                dma("scalar", dbg["nf"], nf_s[:], "d_st")

            t_y_x1 = C["d_st"]
            for e in range(NEXP):
                if me("gpsimd"):
                    w("d_sy", M[("idx_done", e)])
                    if "gt_free" in M:
                        w("c_v", M["gt_free"])
                for k in range(NCH):
                    i = (E.dma_gather(gt_s[:, k], xt_tab.ap(),
                                      idxs[e][:, 8 * k:8 * (k + 1)], 128, 128, 1024, transpose=True)
                         if me("gpsimd") else None)
                    inc(i, "d_g", 16)
                    i = (E.dma_gather(gsc_s[:, k:k + 1, :].bitcast(BF16), sct_tab.ap().bitcast(BF16),
                                      idxs[e][:, 8 * k:8 * (k + 1)], 128, 128, 128)
                         if me("gpsimd") else None)
                    inc(i, "d_g", 16)
                t_gath = C["d_g"]
                if me("scalar") and "b2_free" in M:
                    w("c_g", M["b2_free"])
                tok_b2 = dma("scalar", b2row[:], b2_d[e:e + 1, :], "d_s")
                if me("gpsimd"):
                    w("d_s", tok_b2)
                    if "b2b_free" in M:
                        w("c_v", M["b2b_free"])
                    i = E.partition_broadcast(b2b[:], b2row[0:1, :], 128)
                else:
                    i = None
                t_b2b = inc(i, "c_g")
                M["b2_free"] = t_b2b
                for dffc in range(4):
                    wslot = dffc % 2
                    if me("gpsimd") and ("w1_free", wslot) in M:
                        w("c_t", M[("w1_free", wslot)])
                    tok_w1 = dma("gpsimd", w1c[:, wslot, :, :],
                                 w1_d[e].rearrange("(ft p) f -> p ft f", p=128)[:, :, dffc * 1024:(dffc + 1) * 1024],
                                 "d_g")
                    for dt in range(8):
                        pslot = dt % 2
                        if me("tensor"):
                            if dffc == 0 and dt == 0:
                                w("d_g", t_gath)
                            w("d_g", tok_w1)
                            if ("h1_free", pslot) in M:
                                w("c_v", M[("h1_free", pslot)])
                        i = None
                        for ft in range(8):
                            if me("tensor"):
                                i = E.matmul(ps_a[:, pslot * 512:pslot * 512 + CAP],
                                             w1c[:, wslot, ft, dt * 128:(dt + 1) * 128],
                                             gt_s[:, :, ft, :],
                                             start=(ft == 0), stop=(ft == 7))
                        t_h1 = inc(i, "c_t")
                        M[("w1_free", wslot)] = t_h1
                        if me("vector"):
                            w("c_t", t_h1)
                            i = E.tensor_scalar(h1T[:, dffc * 8 + dt, :], ps_a[:, pslot * 512:pslot * 512 + CAP],
                                                b1_s[:, e, dffc * 8 + dt:dffc * 8 + dt + 1], 0.0,
                                                ALU.add, ALU.max)
                        else:
                            i = None
                        M[("h1_free", pslot)] = inc(i, "c_v")
                t_h1all = C["c_v"]
                eo_slice = [ps_a[:, 0:1024], ps_a[:, 1024:2048], ps_b[:, 0:1024]]
                for dffc in range(4):
                    wslot = dffc % 2
                    if me("gpsimd") and ("w2_free", wslot) in M:
                        w("c_t", M[("w2_free", wslot)])
                    tok_w2 = dma("gpsimd", w2c[:, wslot, :, :],
                                 w2_d[e].rearrange("(c ft p) f -> c p ft f", p=128, ft=8)[dffc],
                                 "d_g")
                    i = None
                    for dt in range(8):
                        d32 = dffc * 8 + dt
                        for ts in range(NCH):
                            for ch in range(2):
                                if me("tensor"):
                                    if dffc == 0 and dt == 0 and ts == 0 and ch == 0:
                                        w("c_v", t_h1all)
                                        if "eo_free" in M:
                                            w("c_v", M["eo_free"])
                                    w("d_g", tok_w2)
                                    i = E.matmul(eo_slice[ts][:, ch * 512:(ch + 1) * 512],
                                                 h1T[:, d32, ts * 128:(ts + 1) * 128],
                                                 w2c[:, wslot, dt, ch * 512:(ch + 1) * 512],
                                                 start=(d32 == 0), stop=(d32 == 31))
                    t_eo = inc(i, "c_t")
                    M[("w2_free", wslot)] = t_eo
                for ts in range(NCH):
                    if me("vector"):
                        w("c_t", t_eo)
                        w("c_g", t_b2b)
                        w("d_g", t_gath)
                        if ("ctr_free", ts) in M:
                            w("d_g", M[("ctr_free", ts)])
                        E.tensor_add(contrib[:, ts, :], b2b[:], eo_slice[ts])
                        i = E.tensor_scalar_mul(contrib[:, ts, :], contrib[:, ts, :], gsc_s[:, ts, e:e + 1])
                    else:
                        i = None
                    inc(i, "c_v")
                t_ctr = C["c_v"]
                M["eo_free"] = t_ctr
                M["gt_free"] = t_ctr
                M["b2b_free"] = t_ctr
                if me("gpsimd"):
                    w("c_v", t_ctr)
                    if e == 0:
                        w("d_st", t_y_x1)
                    else:
                        w("d_g", M["scat_done"])
                for k in range(NCH):
                    i = (E.dma_scatter_add(y_d, contrib[:, k:k + 1, :],
                                           idxs[e][:, 8 * k:8 * (k + 1)], 128, 128, 1024)
                         if me("gpsimd") else None)
                    tk = inc(i, "d_g", 16)
                    M[("ctr_free", k)] = tk
                M["scat_done"] = C["d_g"]

            if me("sync"):
                for sname in ("d_st", "d_s", "d_g", "d_sy"):
                    w(sname, C[sname])

        for eng, starter in (("sync", blk.sync), ("scalar", blk.scalar), ("vector", blk.vector),
                             ("tensor", blk.tensor), ("gpsimd", blk.gpsimd)):
            starter(lambda E, eng=eng: program(eng, E))
    nc.finalize()
    return nc


def make_inputs(inp, core):
    iota = np.zeros((128, 64), np.int16)
    for p in range(128):
        for c in range(64):
            iota[p, c] = (c // 16) * 2048 + p * 16 + (c % 16)
    pos = (np.arange(16)[:, None] + np.arange(CAP // 16)[None, :] * 16).astype(np.float32)
    inp = {k: np.asarray(v) for k, v in inp.items()}
    return {
        "x": np.ascontiguousarray(inp["L0_attn_x"][core * NB:(core + 1) * NB], dtype=np.float32),
        "wq": inp["L0_attn_Wq"], "wk": inp["L0_attn_Wk"],
        "wv": inp["L0_attn_Wv"], "wo": inp["L0_attn_Wo"],
        "rw": inp["L0_ffn_router_w"], "rb": np.ascontiguousarray(inp["L0_ffn_router_b"].reshape(8, 1)),
        "w1": inp["L0_ffn_w1"],
        "b1": np.ascontiguousarray(np.asarray(inp["L0_ffn_b1"]).reshape(8, 32, 128).transpose(2, 0, 1)),
        "w2": inp["L0_ffn_w2"], "b2": inp["L0_ffn_b2"],
        "id8": np.eye(8, dtype=np.float32),
        "iota16": iota, "pos16": pos,
        "esel": np.eye(8, dtype=np.float32)[core:core + 1],
    }


def unshard_y(y):
    return y[:NT].reshape(NB, 128, 16, D).transpose(0, 2, 1, 3).reshape(NB, S, D)


_NC_CACHE = {}


def _get_nc():
    if "nc" not in _NC_CACHE:
        _NC_CACHE["nc"] = build(debug=False, do_ffn=True, nb=NB)
    return _NC_CACHE["nc"]


def kernel(**inputs):
    from concourse.bass_utils import run_bass_kernel_spmd
    nc = _get_nc()
    in_maps = [make_inputs(inputs, c) for c in range(8)]
    res = run_bass_kernel_spmd(nc, in_maps, core_ids=list(range(8)))
    y = np.concatenate([unshard_y(res.results[c]["y"]) for c in range(8)], 0)
    return y


# version 04
# speedup vs baseline: 1.7913x; 1.1006x over previous
import numpy as np
import concourse.bacc as bacc
import concourse.bass as bass
import concourse.mybir as mybir
import concourse.hw_specs as hw_specs
from contextlib import ExitStack

_orig_get_act_tables = hw_specs.get_activation_tables
_AFT = mybir.ActivationFunctionType


def _patched_act_tables(arch):
    out = {}
    for name, funcs in _orig_get_act_tables(arch).items():
        f = set(funcs)
        if name != "natural_log_exp_and_others":
            f.discard(_AFT.Exp)
            f.discard(_AFT.Ln)
        out[name] = f
    return out


hw_specs.get_activation_tables = _patched_act_tables

F32 = mybir.dt.float32
BF16 = mybir.dt.bfloat16
I16 = mybir.dt.int16
U8 = mybir.dt.uint8
U32 = mybir.dt.uint32
ALU = mybir.AluOpType
AF = mybir.ActivationFunctionType
AX = mybir.AxisListType

NB, S, D, H = 4, 2048, 1024, 16
NT = NB * S
NSL = S // 128
CAP = 384
NCH = CAP // 128
DFF = 4096
NEXP = 8
EPS = 1e-5
SCALE = 0.125


def build(debug=False, do_ffn=True, nb=NB):
    nc = bacc.Bacc("TRN2")
    ntok = nb * S
    x_in = nc.dram_tensor("x", [nb, S, D], F32, kind="ExternalInput").ap()
    wq_d = nc.dram_tensor("wq", [D, D], F32, kind="ExternalInput").ap()
    wk_d = nc.dram_tensor("wk", [D, D], F32, kind="ExternalInput").ap()
    wv_d = nc.dram_tensor("wv", [D, D], F32, kind="ExternalInput").ap()
    wo_d = nc.dram_tensor("wo", [D, D], F32, kind="ExternalInput").ap()
    rw_d = nc.dram_tensor("rw", [D, NEXP], F32, kind="ExternalInput").ap()
    rb_d = nc.dram_tensor("rb", [NEXP, 1], F32, kind="ExternalInput").ap()
    w1_d = nc.dram_tensor("w1", [NEXP, D, DFF], F32, kind="ExternalInput").ap()
    b1_d = nc.dram_tensor("b1", [128, NEXP, 32], F32, kind="ExternalInput").ap()
    w2_d = nc.dram_tensor("w2", [NEXP, DFF, D], F32, kind="ExternalInput").ap()
    b2_d = nc.dram_tensor("b2", [NEXP, D], F32, kind="ExternalInput").ap()
    id8_d = nc.dram_tensor("id8", [8, 8], F32, kind="ExternalInput").ap()
    iota_d = nc.dram_tensor("iota16", [128, nb * 16], I16, kind="ExternalInput").ap()
    pos_d = nc.dram_tensor("pos16", [16, CAP // 16], F32, kind="ExternalInput").ap()
    esel_d = nc.dram_tensor("esel", [1, NEXP], F32, kind="ExternalInput").ap()

    y_d = nc.dram_tensor("y", [ntok + 128, D], F32, kind="ExternalOutput").ap()
    dbg = {}
    if debug:
        dbg["xnT"] = nc.dram_tensor("dbg_xnT", [128, 8, S], BF16, kind="ExternalOutput").ap()
        dbg["sc"] = nc.dram_tensor("dbg_sc", [128, nb * 16, 8], F32, kind="ExternalOutput").ap()
        dbg["tau"] = nc.dram_tensor("dbg_tau", [128, 8], F32, kind="ExternalOutput").ap()
        dbg["nf"] = nc.dram_tensor("dbg_nf", [1, NEXP], U32, kind="ExternalOutput").ap()
        if nb == 1:
            dbg["qt"] = nc.dram_tensor("dbg_qt", [128, S], BF16, kind="ExternalOutput").ap()
            dbg["kt"] = nc.dram_tensor("dbg_kt", [128, S], BF16, kind="ExternalOutput").ap()
            dbg["vaug"] = nc.dram_tensor("dbg_vaug", [128, 16, 80], BF16, kind="ExternalOutput").ap()
            dbg["attnT"] = nc.dram_tensor("dbg_attnT", [128, 8, S], BF16, kind="ExternalOutput").ap()
            dbg["lrow"] = nc.dram_tensor("dbg_lrow", [1, 4096], F32, kind="ExternalOutput").ap()
            dbg["ln"] = nc.dram_tensor("dbg_ln", [128, 8], F32, kind="ExternalOutput").ap()
            dbg["lnr16"] = nc.dram_tensor("dbg_lnr16", [128, 16], F32, kind="ExternalOutput").ap()
            dbg["lnm16"] = nc.dram_tensor("dbg_lnm16", [128, 16], F32, kind="ExternalOutput").ap()
            dbg["lnv16"] = nc.dram_tensor("dbg_lnv16", [128, 16], F32, kind="ExternalOutput").ap()
            dbg["xn0"] = nc.dram_tensor("dbg_xn0", [128, D], BF16, kind="ExternalOutput").ap()

    xt_tab = nc.dram_tensor("xt_tab", [ntok + 128, D], BF16)
    sct_tab = nc.dram_tensor("sct_tab", [ntok + 128, 64], F32)
    cc_in = nc.dram_tensor("cc_in", [128, nb * 16, 8], F32)
    cc_out = nc.dram_tensor("cc_out", [8, 128, nb * 16, 8], F32)
    cc_tin = nc.dram_tensor("cc_tin", [1, 1], F32)
    cc_tout = nc.dram_tensor("cc_tout", [8, 1], F32)

    y_rows = y_d[0:ntok, :].rearrange("(b p s) d -> b p s d", b=nb, p=128)
    xt_rows = xt_tab.ap()[0:ntok, :].rearrange("(b p s) d -> b p s d", b=nb, p=128)
    sct_rows = sct_tab.ap()[0:ntok, :].rearrange("(b p s) d -> b p s d", b=nb, p=128)

    ctx = ExitStack()
    with ctx:
        ARENA_BYTES = 206 * 1024
        arena = ctx.enter_context(nc.sbuf_tensor("arena", [128, ARENA_BYTES], U8))
        cursor = [0]

        def carve(nbytes, align=128):
            off = (cursor[0] + align - 1) // align * align
            cursor[0] = off + nbytes
            pass
            return off

        def view(off, shape, dt_):
            esz = {F32: 4, BF16: 2, I16: 2, U8: 1, U32: 4}[dt_]
            n = int(np.prod(shape[1:])) * esz
            ap = arena[0:shape[0], off:off + n].bitcast(dt_)
            if len(shape) == 2:
                return ap
            if len(shape) == 3:
                return ap.rearrange("p (a b) -> p a b", a=shape[1])
            if len(shape) == 4:
                return ap.rearrange("p (a b c) -> p a b c", a=shape[1], b=shape[2])
            raise ValueError

        import os as _os
        def alloc(shape, dt_, _dbg=[0]):
            esz = {F32: 4, BF16: 2, I16: 2, U8: 1, U32: 4}[dt_]
            off = carve(int(np.prod(shape[1:])) * esz)
            if _os.environ.get("ARENA_DBG"):
                print("alloc", shape, dt_, "->", off, cursor[0])
            return view(off, shape, dt_), off

        iota_s, _ = alloc([128, nb * 16], I16)
        pos_s, _ = alloc([16, CAP // 16], F32)
        id8_s, _ = alloc([8, 8], F32)
        rb_s, _ = alloc([8, 1], F32)
        b1_s, _ = alloc([128, NEXP, 32], F32)
        scores_all, _ = alloc([128, nb * 16, 8], F32)
        idxs = []
        for e in range(NEXP):
            t, _ = alloc([128, CAP // 16], I16)
            idxs.append(t)
        nf_s, _ = alloc([1, NEXP], U32)
        kth_s, _ = alloc([1, 16], F32)
        tau_b, _ = alloc([128, 8], F32)
        esel_s, _ = alloc([1, NEXP], F32)
        esel_b, _ = alloc([128, NEXP], F32)
        tau8, _ = alloc([1, 8], F32)
        ones64, _ = alloc([1, 64], F32)
        epst, _ = alloc([128, 1], F32)
        if debug:
            dbgln, _ = alloc([128, 8], F32)
            dbgr16, _ = alloc([128, 16], F32)
            dbgm16, _ = alloc([128, 16], F32)
            dbgv16, _ = alloc([128, 16], F32)

        base = cursor[0]

        wq_s, _ = alloc([128, 8, D], BF16)
        wk_s, _ = alloc([128, 8, D], BF16)
        wv_s, _ = alloc([128, 8, D], BF16)
        wo_s, _ = alloc([128, 8, D], BF16)
        rw_s, _ = alloc([128, 8, 8], BF16)
        xnT, o_xnT = alloc([128, 8, S], BF16)
        attnT, o_attnT = alloc([128, 8, S], BF16)
        qt, o_qt = alloc([128, S], BF16)
        kt_, _ = alloc([128, S], BF16)
        vt, _ = alloc([128, S], BF16)
        vaug0, _ = alloc([128, 16, 80], BF16)
        vaug1, _ = alloc([128, 16, 80], BF16)
        pt, _ = alloc([128, 2, 1024], BF16)
        xt3, _ = alloc([128, 3, D], F32)
        x1t, _ = alloc([128, 2, D], F32)
        xn2b, _ = alloc([128, 2, D], BF16)
        stg_odd, o_stg = alloc([64, 2, 1024], BF16)
        sqd = pt[:, 0, :]
        ssum, _ = alloc([128, 2], F32)
        sumsq, _ = alloc([128, 2], F32)
        lnm, _ = alloc([128, 2], F32)
        lnv, _ = alloc([128, 2], F32)
        lnr, _ = alloc([128, 2], F32)
        sub = cursor[0]
        Lq = arena[64:65, o_stg:o_stg + 4096 * 4].bitcast(F32)
        linv, _ = alloc([1, 4096], F32)
        Lp2, _ = alloc([128, 32], F32)
        lb_sb, _ = alloc([64, 1024], F32)
        l_end = cursor[0]
        cursor[0] = sub
        lg_s, _ = alloc([8, S], F32)
        se_t, _ = alloc([128, 2, 8], F32)
        sden, _ = alloc([128, 2], F32)
        cursor[0] = max(cursor[0], l_end)
        attn_end = cursor[0]

        cursor[0] = base
        scg, _ = alloc([128, 512, 8], F32)
        w1c, o_w1c = alloc([128, 2, 8, 1024], BF16)
        w2c, _ = alloc([128, 2, 8, 1024], BF16)
        h1T, _ = alloc([128, 32, CAP], BF16)
        gt_s, _ = alloc([128, NCH, 8, 128], BF16)
        gsc_s, _ = alloc([128, NCH, 64], F32)
        contrib, _ = alloc([128, NCH, D], F32)
        b2b, _ = alloc([128, D], F32)
        b2row, _ = alloc([1, D], F32)
        msk8, _ = alloc([128, nb * 16], U8)
        sel16, _ = alloc([128, nb * 16], I16)
        neg16, _ = alloc([128, nb * 16], I16)
        selR, _ = alloc([16, nb * 128], I16)
        nff, _ = alloc([1, 1], F32)
        nfb, _ = alloc([16, 1], F32)
        pmask, _ = alloc([16, CAP // 16], U8)
        idum, _ = alloc([16, CAP // 16], I16)
        my_sc = view(o_w1c, [128, 512], F32)
        mskf = view(o_w1c + 2048, [128, 512], F32)
        bis = view(o_w1c + 4096, [128, 8], F32)
        cntp = view(o_w1c + 4160, [128, 1], F32)
        onem = view(o_w1c + 4224, [128, 128], F32)
        ffn_end = cursor[0]
        print(f"arena: attn={attn_end} ffn={ffn_end} cap={ARENA_BYTES}")
        assert max(attn_end, ffn_end) <= ARENA_BYTES

        ps_a = ctx.enter_context(nc.psum_tensor("ps_a", [128, 2048], F32))
        ps_b = ctx.enter_context(nc.psum_tensor("ps_b", [128, 2048], F32))

        sem_names = ["d_g", "d_s", "d_sy", "d_st", "c_v", "c_s", "c_t", "c_g", "c_cc"]
        sems = {n: ctx.enter_context(nc.semaphore(n)) for n in sem_names}
        blk = ctx.enter_context(nc.Block())

        def program(eng, E):
            C = {n: 0 for n in sem_names}
            M = {}

            def me(name):
                return eng == name

            def w(sem, val):
                E.wait_ge(sems[sem], val)

            def inc(inst, sem, n=1):
                if inst is not None:
                    inst.then_inc(sems[sem], n)
                C[sem] += n
                return C[sem]

            def dma(q, out, in_, sem, transpose=False, slow=False):
                i = (E.dma_start(out=out, in_=in_, transpose=transpose,
                                 allow_slow_non_contiguous=slow) if me(q) else None)
                return inc(i, sem, 16)

            for (dst, src) in ((wq_s, wq_d), (wk_s, wk_d), (wv_s, wv_d), (wo_s, wo_d)):
                dma("gpsimd", dst[:], src.rearrange("(ft p) d -> p ft d", p=128), "d_g")
            dma("gpsimd", rw_s[:], rw_d.rearrange("(ft p) e -> p ft e", p=128), "d_g")
            t_wload = dma("gpsimd", b1_s[:], b1_d, "d_g")
            dma("scalar", rb_s[:], rb_d, "d_s")
            dma("scalar", id8_s[:], id8_d, "d_s")
            dma("scalar", iota_s[:], iota_d, "d_s")
            dma("scalar", pos_s[:], pos_d, "d_s")
            t_cload = dma("scalar", esel_s[:], esel_d, "d_s")
            if me("vector"):
                E.memset(ones64[:], 1.0)
                E.memset(epst, EPS)
                E.memset(vaug0[:, :, 64:65], 1.0)
                E.memset(vaug1[:, :, 64:65], 1.0)
                i = E.memset(sqd, 0.0)
            else:
                i = None
            t_init = inc(i, "c_v")
            if me("scalar"):
                w("c_v", t_init)
            dma("scalar", xt_tab.ap()[ntok:ntok + 128, :], sqd, "d_s")
            t_ztab = dma("scalar", sct_tab.ap()[ntok:ntok + 128, :], sqd.bitcast(F32)[:, 0:64], "d_s")
            M_zt = t_ztab

            def emit_ln_stats(xsrc, lnslot):
                if me("vector"):
                    E.tensor_mul(sqd, xsrc, xsrc)
                    E.reduce_sum(sumsq[:, lnslot:lnslot + 1], sqd, axis=AX.X)
                    i = E.reduce_sum(ssum[:, lnslot:lnslot + 1], xsrc, axis=AX.X)
                else:
                    i = None
                t_red = inc(i, "c_v")
                if me("vector"):
                    w("c_v", t_red)
                    if ("lnv_free", lnslot) in M:
                        w("c_s", M[("lnv_free", lnslot)])
                    i = E.tensor_scalar_mul(lnm[:, lnslot:lnslot + 1], ssum[:, lnslot:lnslot + 1], 1.0 / D)
                else:
                    i = None
                tt1 = inc(i, "c_v")
                if me("vector"):
                    w("c_v", tt1)
                    i = E.tensor_mul(lnv[:, lnslot:lnslot + 1], lnm[:, lnslot:lnslot + 1], lnm[:, lnslot:lnslot + 1])
                else:
                    i = None
                tt2 = inc(i, "c_v")
                if me("vector"):
                    w("c_v", tt2)
                    i = E.scalar_tensor_tensor(lnv[:, lnslot:lnslot + 1], sumsq[:, lnslot:lnslot + 1],
                                               1.0 / D, lnv[:, lnslot:lnslot + 1], ALU.mult, ALU.subtract)
                else:
                    i = None
                t_stats = inc(i, "c_v")
                if me("scalar"):
                    w("c_v", t_stats)
                    E.activation(lnr[:, lnslot:lnslot + 1], lnv[:, lnslot:lnslot + 1], AF.Ln, bias=epst)
                    i = E.activation(lnr[:, lnslot:lnslot + 1], lnr[:, lnslot:lnslot + 1], AF.Exp, scale=-0.5)
                else:
                    i = None
                t_r = inc(i, "c_s")
                M[("lnv_free", lnslot)] = t_r
                return t_r

            def emit_ln_apply(xsrc, lnslot, xn_dst, t_r):
                if me("vector"):
                    w("c_s", t_r)
                    i = E.tensor_scalar(xn_dst, xsrc, lnm[:, lnslot:lnslot + 1],
                                        lnr[:, lnslot:lnslot + 1], ALU.subtract, ALU.mult)
                else:
                    i = None
                return inc(i, "c_v")

            def emit_ln(xsrc, lnslot, xn_dst):
                return emit_ln_apply(xsrc, lnslot, xn_dst, emit_ln_stats(xsrc, lnslot))

            def emit_division(hp):
                t_all_ev = M[("pair_ev", hp)]
                if me("sync"):
                    w("c_v", t_all_ev)
                tok = dma("sync", Lp2[:], Lq[0:1, :], "d_sy")
                if me("vector"):
                    w("d_sy", tok)
                    i = E.reciprocal(Lp2[:], Lp2[:])
                else:
                    i = None
                t_rc = inc(i, "c_v")
                if me("sync"):
                    w("c_v", t_rc)
                t_linv = dma("sync", linv[:], Lp2[:], "d_sy")
                for u in range(4):
                    hh, qc = u // 2, u % 2
                    if me("gpsimd"):
                        w("d_sy", t_linv)
                        if "lbg" in M:
                            w("c_v", M["lbg"])
                        i = E.partition_broadcast(lb_sb[:],
                                                  linv[0:1, u * 1024:(u + 1) * 1024], 64)
                    else:
                        i = None
                    t_lb = inc(i, "c_g")
                    if me("vector"):
                        w("c_g", t_lb)
                        if hh == 0:
                            sl = attnT[0:64, hp, qc * 1024:(qc + 1) * 1024]
                        else:
                            sl = stg_odd[:, qc, :]
                        i = E.tensor_mul(sl, sl, lb_sb[:])
                    else:
                        i = None
                    t_div = inc(i, "c_v")
                    M["lbg"] = t_div
                M[("ot_ev", 0)] = max(M.get(("ot_ev", 0), 0), t_div)
                M[("ot_ev", 1)] = max(M.get(("ot_ev", 1), 0), t_div)
                if me("sync"):
                    w("c_v", t_div)
                for qc in range(2):
                    dma("sync", attnT[64:128, hp, qc * 1024:(qc + 1) * 1024], stg_odd[:, qc, :], "d_sy")
                M[("div_done", hp)] = t_div
                M["stg_free"] = C["d_sy"]

            for b in range(nb):
                ln_tr = {}
                for s in range(NSL + 1):
                    if s < NSL:
                        xslot, lnslot = s % 3, s % 2
                        if me("scalar") and ("x_free", xslot) in M:
                            w("c_v", M[("x_free", xslot)])
                        tok = dma("scalar", xt3[:, xslot, :], x_in[b, s * 128:(s + 1) * 128, :], "d_s")
                        if me("vector"):
                            w("d_s", max(tok, M_zt))
                        ln_tr[s] = emit_ln_stats(xt3[:, xslot, :], lnslot)
                    if s >= 1:
                        sp = s - 1
                        xslot, lnslot = sp % 3, sp % 2
                        if me("vector") and ("xn_free", lnslot) in M:
                            w("d_sy", M[("xn_free", lnslot)])
                        t_xn = emit_ln_apply(xt3[:, xslot, :], lnslot, xn2b[:, lnslot, :], ln_tr[sp])
                        M[("x_free", xslot)] = t_xn
                        if me("sync"):
                            w("c_v", t_xn)
                        dma("sync", xnT[:, :, sp * 128:(sp + 1) * 128],
                            xn2b[:, lnslot, :], "d_sy", transpose=True)
                        M[("xn_free", lnslot)] = C["d_sy"]
                t_xnT = C["d_sy"]
                if debug and b == 0 and nb == 1:
                    if me("sync"):
                        w("d_sy", t_xnT)
                        w("c_v", C["c_v"])
                    dma("sync", dbg["xnT"], xnT[:], "d_st")
                    dma("sync", dbg["lnr16"], dbgr16[:], "d_st")
                    dma("sync", dbg["lnm16"], dbgm16[:], "d_st")
                    dma("sync", dbg["lnv16"], dbgv16[:], "d_st")

                for hp in range(8):
                    for mi, (wmat, dst) in enumerate(((wv_s, vt), (wq_s, qt), (wk_s, kt_))):
                        if me("tensor"):
                            w("d_sy", t_xnT)
                            w("d_g", t_wload)
                            cvw = max(M.get(("proj_free", 0), 0), M.get(("proj_free", 1), 0))
                            if cvw:
                                w("c_v", cvw)
                            csw = max(M.get("st_free", 0), M.get("router_ps_free", 0))
                            if csw:
                                w("c_s", csw)
                        t_ch = {}
                        i = None
                        for ft in range(8):
                            for ch in range(4):
                                if me("tensor"):
                                    if ft == 0 and ("qkv_ev_ch", ch) in M:
                                        w("c_v", M[("qkv_ev_ch", ch)])
                                    i = E.matmul(ps_a[:, ch * 512:(ch + 1) * 512],
                                                 wmat[:, ft, hp * 128:(hp + 1) * 128],
                                                 xnT[:, ft, ch * 512:(ch + 1) * 512],
                                                 start=(ft == 0), stop=(ft == 7))
                                if ft == 7:
                                    t_ch[ch] = inc(i, "c_t")
                        for ch in range(4):
                            if me("vector"):
                                w("c_t", t_ch[ch])
                                i = E.tensor_copy(dst[:, ch * 512:(ch + 1) * 512], ps_a[:, ch * 512:(ch + 1) * 512])
                            else:
                                i = None
                            M[("qkv_ev_ch", ch)] = inc(i, "c_v")
                        M["qkv_free"] = M[("qkv_ev_ch", 3)]
                        if mi == 0:
                            if me("sync"):
                                w("c_v", M["qkv_free"])
                            t_vaug_kt = {}
                            dma("sync", vaug0[:, :, 0:64], vt[0:64, :], "d_sy", transpose=True)
                            tvv = dma("sync", vaug1[:, :, 0:64], vt[64:128, :], "d_sy", transpose=True)
                            for ktile in range(16):
                                t_vaug_kt[ktile] = tvv
                    t_qkv = M["qkv_free"]
                    t_vaug = t_vaug_kt[15]
                    if hp > 0:
                        emit_division(hp - 1)

                    if debug and nb == 1 and b == 0 and hp == 0:
                        if me("sync"):
                            w("d_sy", t_vaug)
                        dma("sync", dbg["qt"], qt[:], "d_st")
                        dma("sync", dbg["kt"], kt_[:], "d_st")
                        dma("sync", dbg["vaug"], vaug0[:].rearrange("p a b -> p (a b)").rearrange("p (a b) -> p a b", a=16)[:, :, 0:80], "d_st")
                    for hh in range(2):
                        hbase = 64 * hh
                        for qc in range(2):
                            u = hh * 2 + qc
                            otp = u % 2
                            t_exp_of = {}
                            for step in range(17):
                                if step < 16:
                                    ktile = step
                                    stp = ktile % 2
                                    if me("tensor"):
                                        if ktile < 2:
                                            w("c_v", t_qkv)
                                        if ("exp", stp) in M:
                                            w("c_s", M[("exp", stp)])
                                    i = None
                                    for n in range(2):
                                        if me("tensor"):
                                            i = E.matmul(ps_a[:, stp * 1024 + n * 512: stp * 1024 + (n + 1) * 512],
                                                         kt_[hbase:hbase + 64, ktile * 128:(ktile + 1) * 128],
                                                         qt[hbase:hbase + 64, qc * 1024 + n * 512: qc * 1024 + (n + 1) * 512],
                                                         start=True, stop=True)
                                    t_st = inc(i, "c_t")
                                    if me("scalar"):
                                        w("c_t", t_st)
                                        if ("pt_free", stp) in M:
                                            w("c_t", M[("pt_free", stp)])
                                        i = E.activation(pt[:, stp, :], ps_a[:, stp * 1024:(stp + 1) * 1024],
                                                         AF.Exp, scale=SCALE)
                                    else:
                                        i = None
                                    t_exp = inc(i, "c_s")
                                    M[("exp", stp)] = t_exp
                                    t_exp_of[ktile] = t_exp
                                    if ktile >= 14:
                                        M["st_free"] = t_exp
                                if step >= 1:
                                    ktile = step - 1
                                    stp = ktile % 2
                                    i = None
                                    for n in range(2):
                                        if me("tensor"):
                                            w("c_s", t_exp_of[ktile])
                                            if n == 0:
                                                w("d_sy", t_vaug_kt[ktile])
                                            if ktile == 0 and n == 0 and ("ot_ev", otp) in M:
                                                w("c_v", M[("ot_ev", otp)])
                                            i = E.matmul(ps_b[0:65, otp * 1024 + n * 512: otp * 1024 + (n + 1) * 512],
                                                         (vaug0 if hh == 0 else vaug1)[:, ktile, 0:65],
                                                         pt[:, stp, n * 512:(n + 1) * 512],
                                                         start=(ktile == 0), stop=(ktile == 15))
                                    t_pv = inc(i, "c_t")
                                    M[("pt_free", stp)] = t_pv
                            if me("vector"):
                                w("c_t", t_pv)
                                if hh == 1 and "stg_free" in M:
                                    w("d_sy", M["stg_free"])
                                E.tensor_copy(Lq[0:1, u * 1024:(u + 1) * 1024],
                                              ps_b[64:65, otp * 1024:(otp + 1) * 1024])
                                if hh == 0:
                                    i = E.tensor_copy(attnT[0:64, hp, qc * 1024:(qc + 1) * 1024],
                                                      ps_b[0:64, otp * 1024:(otp + 1) * 1024])
                                else:
                                    i = E.tensor_copy(stg_odd[:, qc, :],
                                                      ps_b[0:64, otp * 1024:(otp + 1) * 1024])
                            else:
                                i = None
                            M[("ot_ev", otp)] = inc(i, "c_v")
                    M[("pair_ev", hp)] = C["c_v"]
                emit_division(7)
                if debug and nb == 1 and b == 0:
                    tok = C["d_sy"]
                    if me("sync"):
                        w("d_sy", tok)
                    dma("sync", dbg["attnT"], attnT[:], "d_st")
                    dma("sync", dbg["lrow"], Lq[0:1, :], "d_st")

                t_attnT = C["d_sy"]
                for s in range(NSL):
                    xslot, lnslot = s % 2, s % 2
                    if me("scalar") and ("x_free", xslot) in M:
                        w("c_v", M[("x_free", xslot)])
                    tok_x = dma("scalar", xt3[:, xslot, :], x_in[b, s * 128:(s + 1) * 128, :], "d_s")
                    if me("tensor"):
                        w("d_sy", t_attnT)
                        if ("proj_free", s % 2) in M:
                            w("c_v", M[("proj_free", s % 2)])
                    i = None
                    for ft in range(8):
                        for ch in range(2):
                            if me("tensor"):
                                i = E.matmul(ps_a[:, (s % 2) * 1024 + ch * 512:(s % 2) * 1024 + (ch + 1) * 512],
                                             attnT[:, ft, s * 128:(s + 1) * 128],
                                             wo_s[:, ft, ch * 512:(ch + 1) * 512],
                                             start=(ft == 0), stop=(ft == 7))
                    t_proj = inc(i, "c_t")
                    if me("vector"):
                        w("c_t", t_proj)
                        w("d_s", tok_x)
                        if ("x1_free", s % 2) in M:
                            w("d_st", M[("x1_free", s % 2)])
                        i = E.tensor_add(x1t[:, s % 2, :], xt3[:, xslot, :],
                                         ps_a[:, (s % 2) * 1024:(s % 2) * 1024 + 1024])
                    else:
                        i = None
                    t_x1 = inc(i, "c_v")
                    M[("proj_free", s % 2)] = t_x1
                    if me("scalar"):
                        w("c_v", t_x1)
                    tok_y = dma("scalar", y_rows[b, :, s, :], x1t[:, s % 2, :], "d_st")
                    if me("vector"):
                        if ("xn_free", lnslot) in M:
                            w("d_sy", M[("xn_free", lnslot)])
                        if ("xn2_free", lnslot) in M:
                            w("d_st", M[("xn2_free", lnslot)])
                    t_xn = emit_ln(x1t[:, s % 2, :], lnslot, xn2b[:, lnslot, :])
                    M[("x_free", xslot)] = t_xn
                    M[("x1_free", s % 2)] = tok_y
                    if me("scalar"):
                        w("c_v", t_xn)
                    dma("scalar", xt_rows[b, :, s, :], xn2b[:, lnslot, :], "d_st")
                    M[("xn2_free", lnslot)] = C["d_st"]
                    if me("sync"):
                        w("c_v", t_xn)
                    dma("sync", xnT[:, :, s * 128:(s + 1) * 128],
                        xn2b[:, lnslot, :], "d_sy", transpose=True)
                    M[("xn_free", lnslot)] = C["d_sy"]

                t_xnT2 = C["d_sy"]
                if me("tensor"):
                    w("d_sy", t_xnT2)
                    w("c_v", M[("div_done", 7)])
                    if "lg_free" in M:
                        w("c_v", M["lg_free"])
                i = None
                for ft in range(8):
                    for ch in range(4):
                        if me("tensor"):
                            i = E.matmul(ps_b[0:8, ch * 512:(ch + 1) * 512],
                                         rw_s[:, ft, 0:8],
                                         xnT[:, ft, ch * 512:(ch + 1) * 512],
                                         start=(ft == 0), stop=(ft == 7))
                t_lg = inc(i, "c_t")
                if me("vector"):
                    w("c_t", t_lg)
                    w("d_s", t_cload)
                    i = E.tensor_scalar(lg_s[:], ps_b[0:8, :], rb_s[0:8, 0:1], None, ALU.add)
                else:
                    i = None
                t_lgs = inc(i, "c_v")
                M["lg_free"] = t_lgs
                for s in range(NSL):
                    if me("tensor"):
                        w("c_v", t_lgs)
                        if ("se_free", s % 2) in M:
                            w("c_s", M[("se_free", s % 2)])
                        if ("sc_free", s % 2) in M:
                            w("c_v", M[("sc_free", s % 2)])
                        i = E.transpose(ps_a[:, (s % 2) * 512:(s % 2) * 512 + 8],
                                        lg_s[0:8, s * 128:(s + 1) * 128], id8_s[:])
                    else:
                        i = None
                    t_tp = inc(i, "c_t")
                    if me("scalar"):
                        w("c_t", t_tp)
                        if ("sc_free", s % 2) in M:
                            w("c_v", M[("sc_free", s % 2)])
                        i = E.activation(se_t[:, s % 2, :], ps_a[:, (s % 2) * 512:(s % 2) * 512 + 8],
                                         AF.Exp, accum_out=sden[:, s % 2:s % 2 + 1])
                    else:
                        i = None
                    t_se = inc(i, "c_s")
                    M[("se_free", s % 2)] = t_se
                    if me("vector"):
                        w("c_s", t_se)
                        i = E.reciprocal(sden[:, s % 2:s % 2 + 1], sden[:, s % 2:s % 2 + 1])
                    else:
                        i = None
                    t_rd = inc(i, "c_v")
                    if me("vector"):
                        w("c_v", t_rd)
                        i = E.tensor_scalar_mul(scores_all[:, b * 16 + s, :], se_t[:, s % 2, :],
                                                sden[:, s % 2:s % 2 + 1])
                    else:
                        i = None
                    t_sc = inc(i, "c_v")
                    M[("sc_free", s % 2)] = t_sc
                    M["router_ps_free"] = t_se
                t_scb = C["c_v"]
                M[("ot_ev", 0)] = max(M.get(("ot_ev", 0), 0), t_lgs)
                M[("ot_ev", 1)] = max(M.get(("ot_ev", 1), 0), t_lgs)
                if me("scalar"):
                    w("c_v", t_scb)
                dma("scalar", sct_rows[b][:, :, 0:8], scores_all[:, b * 16:(b + 1) * 16, :], "d_st")
                dma("scalar", cc_in.ap()[:, b * 16:(b + 1) * 16, :],
                    scores_all[:, b * 16:(b + 1) * 16, :], "d_st")

            if me("vector"):
                w("d_st", C["d_st"])
                i = E.memset(x1t[:, 0, :], 0.0)
            else:
                i = None
            tz = inc(i, "c_v")
            if me("scalar"):
                w("c_v", tz)
            dma("scalar", y_d[ntok:ntok + 128, :], x1t[:, 0, :], "d_st")

            if debug:
                tok = C["c_v"]
                if me("scalar"):
                    w("c_v", tok)
                dma("scalar", dbg["sc"], scores_all[:], "d_st")

            if not do_ffn:
                if me("sync"):
                    for sname in ("d_st", "d_s", "d_g", "d_sy"):
                        w(sname, C[sname])
                return

            t_ccin = C["d_st"]
            if me("gpsimd"):
                w("d_st", t_ccin)
                i = E.collective_compute("AllGather", ALU.bypass,
                                         replica_groups=[list(range(8))],
                                         ins=[cc_in.ap().opt()], outs=[cc_out.ap().opt()])
            else:
                i = None
            t_cc = inc(i, "c_cc")
            if me("scalar"):
                w("c_cc", t_cc)
            for r in range(8):
                dma("scalar", scg[:, r * (nb * 16):(r + 1) * (nb * 16), :], cc_out.ap()[r], "d_s")
            t_scg = C["d_s"]

            if me("gpsimd"):
                w("d_s", t_cload)
                i = E.partition_broadcast(esel_b[:], esel_s[0:1, :], 128)
            else:
                i = None
            t_eb = inc(i, "c_g")
            if me("vector"):
                w("c_g", t_eb)
                w("d_s", t_scg)
                i = E.tensor_scalar_mul(my_sc[:], scg[:, :, 0:1].rearrange("p n o -> p (n o)"),
                                        esel_b[:, 0:1])
            else:
                i = None
            tok = inc(i, "c_v")
            for e in range(1, NEXP):
                if me("vector"):
                    i = E.scalar_tensor_tensor(my_sc[:], scg[:, :, e:e + 1].rearrange("p n o -> p (n o)"),
                                               esel_b[:, e:e + 1], my_sc[:], ALU.mult, ALU.add)
                else:
                    i = None
                tok = inc(i, "c_v")
            t_mysc = tok
            if me("vector"):
                w("c_v", t_mysc)
                E.memset(bis[:, 0:1], 0.0)
                E.memset(bis[:, 1:2], 1.0)
                E.memset(bis[:, 2:3], 0.5)
                i = E.memset(onem[:], 1.0)
            else:
                i = None
            t_bis = inc(i, "c_v")
            for it in range(36):
                if me("vector"):
                    w("c_v", t_bis)
                    E.tensor_scalar(mskf[:], my_sc[:], bis[:, 2:3], None, ALU.is_gt)
                    i = E.reduce_sum(cntp[:], mskf[:], axis=AX.X)
                else:
                    i = None
                t_cnt = inc(i, "c_v")
                if me("tensor"):
                    w("c_v", t_cnt)
                    i = E.matmul(ps_b[0:128, (it % 2) * 512:(it % 2) * 512 + 1],
                                 onem[:], cntp[:], start=True, stop=True)
                else:
                    i = None
                t_tot = inc(i, "c_t")
                if me("vector"):
                    w("c_t", t_tot)
                    i = E.tensor_scalar(bis[:, 4:5], ps_b[0:128, (it % 2) * 512:(it % 2) * 512 + 1],
                                        2048.0, None, ALU.is_ge)
                else:
                    i = None
                tb3 = inc(i, "c_v")
                if me("vector"):
                    w("c_v", tb3)
                    E.tensor_sub(bis[:, 5:6], bis[:, 2:3], bis[:, 0:1])
                    E.tensor_sub(bis[:, 6:7], bis[:, 1:2], bis[:, 2:3])
                    i = E.tensor_scalar(bis[:, 7:8], bis[:, 4:5], -1.0, 1.0, ALU.mult, ALU.add)
                else:
                    i = None
                tb4 = inc(i, "c_v")
                if me("vector"):
                    w("c_v", tb4)
                    E.tensor_mul(bis[:, 5:6], bis[:, 5:6], bis[:, 4:5])
                    i = E.tensor_mul(bis[:, 6:7], bis[:, 6:7], bis[:, 7:8])
                else:
                    i = None
                tb5 = inc(i, "c_v")
                if me("vector"):
                    w("c_v", tb5)
                    E.tensor_add(bis[:, 0:1], bis[:, 0:1], bis[:, 5:6])
                    i = E.tensor_sub(bis[:, 1:2], bis[:, 1:2], bis[:, 6:7])
                else:
                    i = None
                tb6 = inc(i, "c_v")
                if me("vector"):
                    w("c_v", tb6)
                    i = E.tensor_add(bis[:, 3:4], bis[:, 0:1], bis[:, 1:2])
                else:
                    i = None
                tb7 = inc(i, "c_v")
                if me("vector"):
                    w("c_v", tb7)
                    i = E.tensor_scalar_mul(bis[:, 2:3], bis[:, 3:4], 0.5)
                else:
                    i = None
                t_bis = inc(i, "c_v")
            if me("scalar"):
                w("c_v", t_bis)
            tok = dma("scalar", cc_tin.ap(), bis[0:1, 0:1], "d_s")
            if me("gpsimd"):
                w("d_s", tok)
                i = E.collective_compute("AllGather", ALU.bypass,
                                         replica_groups=[list(range(8))],
                                         ins=[cc_tin.ap().opt()], outs=[cc_tout.ap().opt()])
            else:
                i = None
            t_cc2 = inc(i, "c_cc")
            if me("scalar"):
                w("c_cc", t_cc2)
            tok = dma("scalar", tau8[:], cc_tout.ap().rearrange("e o -> o e"), "d_s")
            if me("gpsimd"):
                w("d_s", tok)
                i = E.partition_broadcast(tau_b[:], tau8[0:1, :], 128)
            else:
                i = None
            t_tau = inc(i, "c_g")
            if debug:
                if me("scalar"):
                    w("c_g", t_tau)
                dma("scalar", dbg["tau"], tau_b[:], "d_st")

            for e in range(NEXP):
                if me("vector"):
                    w("c_g", t_tau)
                    if e == 0:
                        E.memset(neg16[:], -1)
                        E.memset(idum[:], NT)
                    if "selR_free" in M:
                        w("c_g", M["selR_free"])
                    E.tensor_scalar(msk8[:], scores_all[:, :, e:e + 1].rearrange("p n o -> p (n o)"),
                                    tau_b[:, e:e + 1], None, ALU.is_gt)
                    E.memset(idxs[e][0:16, :], NT)
                    i = E.select(sel16[:], msk8[:], iota_s[:], neg16[:])
                else:
                    i = None
                t_sel = inc(i, "c_v")
                if me("sync"):
                    w("c_v", t_sel)
                for k in range(8):
                    dma("sync", selR[:, nb * 16 * k: nb * 16 * (k + 1)],
                        sel16[16 * k:16 * (k + 1), :], "d_sy")
                t_selR = C["d_sy"]
                if me("gpsimd"):
                    w("d_sy", t_selR)
                    i = E.sparse_gather(idxs[e][0:16, :], selR[:], num_found=nf_s[0:1, e:e + 1])
                else:
                    i = None
                t_sg = inc(i, "c_g")
                M["selR_free"] = t_sg
                if me("vector"):
                    w("c_g", t_sg)
                    i = E.tensor_copy(nff[:], nf_s[0:1, e:e + 1])
                else:
                    i = None
                t_nff = inc(i, "c_v")
                if me("gpsimd"):
                    w("c_v", t_nff)
                    i = E.partition_broadcast(nfb[:], nff[0:1, :], 16)
                else:
                    i = None
                t_nfb = inc(i, "c_g")
                if me("vector"):
                    w("c_g", t_nfb)
                    E.tensor_scalar(pmask[:], pos_s[:], nfb[:, 0:1], None, ALU.is_ge)
                    i = E.select(idxs[e][0:16, :], pmask[:], idum[:], idxs[e][0:16, :])
                else:
                    i = None
                t_fix = inc(i, "c_v")
                if me("sync"):
                    w("c_v", t_fix)
                tk = dma("sync", idxs[e][16:32, :], idxs[e][0:16, :], "d_sy")
                if me("sync"):
                    w("d_sy", tk)
                tk = dma("sync", idxs[e][32:64, :], idxs[e][0:32, :], "d_sy")
                if me("sync"):
                    w("d_sy", tk)
                tk = dma("sync", idxs[e][64:128, :], idxs[e][0:64, :], "d_sy")
                M[("idx_done", e)] = tk
            if debug:
                tok = C["c_g"]
                if me("scalar"):
                    w("c_g", tok)
                dma("scalar", dbg["nf"], nf_s[:], "d_st")

            t_y_x1 = C["d_st"]
            for e in range(NEXP):
                if me("gpsimd"):
                    w("d_sy", M[("idx_done", e)])
                    if "gt_free" in M:
                        w("c_v", M["gt_free"])
                for k in range(NCH):
                    i = (E.dma_gather(gt_s[:, k], xt_tab.ap(),
                                      idxs[e][:, 8 * k:8 * (k + 1)], 128, 128, 1024, transpose=True)
                         if me("gpsimd") else None)
                    inc(i, "d_g", 16)
                    i = (E.dma_gather(gsc_s[:, k:k + 1, :].bitcast(BF16), sct_tab.ap().bitcast(BF16),
                                      idxs[e][:, 8 * k:8 * (k + 1)], 128, 128, 128)
                         if me("gpsimd") else None)
                    inc(i, "d_g", 16)
                t_gath = C["d_g"]
                if me("scalar") and "b2_free" in M:
                    w("c_g", M["b2_free"])
                tok_b2 = dma("scalar", b2row[:], b2_d[e:e + 1, :], "d_s")
                if me("gpsimd"):
                    w("d_s", tok_b2)
                    if "b2b_free" in M:
                        w("c_v", M["b2b_free"])
                    i = E.partition_broadcast(b2b[:], b2row[0:1, :], 128)
                else:
                    i = None
                t_b2b = inc(i, "c_g")
                M["b2_free"] = t_b2b
                for dffc in range(4):
                    wslot = dffc % 2
                    if me("gpsimd") and ("w1_free", wslot) in M:
                        w("c_t", M[("w1_free", wslot)])
                    tok_w1 = dma("gpsimd", w1c[:, wslot, :, :],
                                 w1_d[e].rearrange("(ft p) f -> p ft f", p=128)[:, :, dffc * 1024:(dffc + 1) * 1024],
                                 "d_g")
                    for dt in range(8):
                        pslot = dt % 2
                        if me("tensor"):
                            if dffc == 0 and dt == 0:
                                w("d_g", t_gath)
                            w("d_g", tok_w1)
                            if ("h1_free", pslot) in M:
                                w("c_v", M[("h1_free", pslot)])
                        i = None
                        for ft in range(8):
                            if me("tensor"):
                                i = E.matmul(ps_a[:, pslot * 512:pslot * 512 + CAP],
                                             w1c[:, wslot, ft, dt * 128:(dt + 1) * 128],
                                             gt_s[:, :, ft, :],
                                             start=(ft == 0), stop=(ft == 7))
                        t_h1 = inc(i, "c_t")
                        M[("w1_free", wslot)] = t_h1
                        if me("vector"):
                            w("c_t", t_h1)
                            i = E.tensor_scalar(h1T[:, dffc * 8 + dt, :], ps_a[:, pslot * 512:pslot * 512 + CAP],
                                                b1_s[:, e, dffc * 8 + dt:dffc * 8 + dt + 1], 0.0,
                                                ALU.add, ALU.max)
                        else:
                            i = None
                        M[("h1_free", pslot)] = inc(i, "c_v")
                t_h1all = C["c_v"]
                eo_slice = [ps_a[:, 0:1024], ps_a[:, 1024:2048], ps_b[:, 0:1024]]
                for dffc in range(4):
                    wslot = dffc % 2
                    if me("gpsimd") and ("w2_free", wslot) in M:
                        w("c_t", M[("w2_free", wslot)])
                    tok_w2 = dma("gpsimd", w2c[:, wslot, :, :],
                                 w2_d[e].rearrange("(c ft p) f -> c p ft f", p=128, ft=8)[dffc],
                                 "d_g")
                    i = None
                    for dt in range(8):
                        d32 = dffc * 8 + dt
                        for ts in range(NCH):
                            for ch in range(2):
                                if me("tensor"):
                                    if dffc == 0 and dt == 0 and ts == 0 and ch == 0:
                                        w("c_v", t_h1all)
                                        if "eo_free" in M:
                                            w("c_v", M["eo_free"])
                                    w("d_g", tok_w2)
                                    i = E.matmul(eo_slice[ts][:, ch * 512:(ch + 1) * 512],
                                                 h1T[:, d32, ts * 128:(ts + 1) * 128],
                                                 w2c[:, wslot, dt, ch * 512:(ch + 1) * 512],
                                                 start=(d32 == 0), stop=(d32 == 31))
                    t_eo = inc(i, "c_t")
                    M[("w2_free", wslot)] = t_eo
                for ts in range(NCH):
                    if me("vector"):
                        w("c_t", t_eo)
                        w("c_g", t_b2b)
                        w("d_g", t_gath)
                        if ("ctr_free", ts) in M:
                            w("d_g", M[("ctr_free", ts)])
                        E.tensor_add(contrib[:, ts, :], b2b[:], eo_slice[ts])
                        i = E.tensor_scalar_mul(contrib[:, ts, :], contrib[:, ts, :], gsc_s[:, ts, e:e + 1])
                    else:
                        i = None
                    inc(i, "c_v")
                t_ctr = C["c_v"]
                M["eo_free"] = t_ctr
                M["gt_free"] = t_ctr
                M["b2b_free"] = t_ctr
                if me("gpsimd"):
                    w("c_v", t_ctr)
                    if e == 0:
                        w("d_st", t_y_x1)
                    else:
                        w("d_g", M["scat_done"])
                for k in range(NCH):
                    i = (E.dma_scatter_add(y_d, contrib[:, k:k + 1, :],
                                           idxs[e][:, 8 * k:8 * (k + 1)], 128, 128, 1024)
                         if me("gpsimd") else None)
                    tk = inc(i, "d_g", 16)
                    M[("ctr_free", k)] = tk
                M["scat_done"] = C["d_g"]

            if me("sync"):
                for sname in ("d_st", "d_s", "d_g", "d_sy"):
                    w(sname, C[sname])

        for eng, starter in (("sync", blk.sync), ("scalar", blk.scalar), ("vector", blk.vector),
                             ("tensor", blk.tensor), ("gpsimd", blk.gpsimd)):
            starter(lambda E, eng=eng: program(eng, E))
    nc.finalize()
    return nc


def make_inputs(inp, core):
    iota = np.zeros((128, 64), np.int16)
    for p in range(128):
        for c in range(64):
            iota[p, c] = (c // 16) * 2048 + p * 16 + (c % 16)
    pos = (np.arange(16)[:, None] + np.arange(CAP // 16)[None, :] * 16).astype(np.float32)
    inp = {k: np.asarray(v) for k, v in inp.items()}
    return {
        "x": np.ascontiguousarray(inp["L0_attn_x"][core * NB:(core + 1) * NB], dtype=np.float32),
        "wq": inp["L0_attn_Wq"], "wk": inp["L0_attn_Wk"],
        "wv": inp["L0_attn_Wv"], "wo": inp["L0_attn_Wo"],
        "rw": inp["L0_ffn_router_w"], "rb": np.ascontiguousarray(inp["L0_ffn_router_b"].reshape(8, 1)),
        "w1": inp["L0_ffn_w1"],
        "b1": np.ascontiguousarray(np.asarray(inp["L0_ffn_b1"]).reshape(8, 32, 128).transpose(2, 0, 1)),
        "w2": inp["L0_ffn_w2"], "b2": inp["L0_ffn_b2"],
        "id8": np.eye(8, dtype=np.float32),
        "iota16": iota, "pos16": pos,
        "esel": np.eye(8, dtype=np.float32)[core:core + 1],
    }


def unshard_y(y):
    return y[:NT].reshape(NB, 128, 16, D).transpose(0, 2, 1, 3).reshape(NB, S, D)


_NC_CACHE = {}


def _get_nc():
    if "nc" not in _NC_CACHE:
        _NC_CACHE["nc"] = build(debug=False, do_ffn=True, nb=NB)
    return _NC_CACHE["nc"]


def kernel(**inputs):
    from concourse.bass_utils import run_bass_kernel_spmd
    nc = _get_nc()
    in_maps = [make_inputs(inputs, c) for c in range(8)]
    res = run_bass_kernel_spmd(nc, in_maps, core_ids=list(range(8)))
    y = np.concatenate([unshard_y(res.results[c]["y"]) for c in range(8)], 0)
    return y


# version 05
# speedup vs baseline: 2.0913x; 1.1674x over previous
import numpy as np
import concourse.bacc as bacc
import concourse.bass as bass
import concourse.mybir as mybir
import concourse.hw_specs as hw_specs
from contextlib import ExitStack

_orig_get_act_tables = hw_specs.get_activation_tables
_AFT = mybir.ActivationFunctionType


def _patched_act_tables(arch):
    out = {}
    for name, funcs in _orig_get_act_tables(arch).items():
        f = set(funcs)
        if name != "natural_log_exp_and_others":
            f.discard(_AFT.Exp)
            f.discard(_AFT.Ln)
        out[name] = f
    return out


hw_specs.get_activation_tables = _patched_act_tables

F32 = mybir.dt.float32
BF16 = mybir.dt.bfloat16
I16 = mybir.dt.int16
U8 = mybir.dt.uint8
U32 = mybir.dt.uint32
ALU = mybir.AluOpType
AF = mybir.ActivationFunctionType
AX = mybir.AxisListType

NB, S, D, H = 4, 2048, 1024, 16
NT = NB * S
NSL = S // 128
CAP = 384
NCH = CAP // 128
DFF = 4096
NEXP = 8
EPS = 1e-5
SCALE = 0.125


def build(debug=False, do_ffn=True, nb=NB):
    nc = bacc.Bacc("TRN2")
    ntok = nb * S
    x_in = nc.dram_tensor("x", [nb, S, D], F32, kind="ExternalInput").ap()
    wq_d = nc.dram_tensor("wq", [D, D], F32, kind="ExternalInput").ap()
    wk_d = nc.dram_tensor("wk", [D, D], F32, kind="ExternalInput").ap()
    wv_d = nc.dram_tensor("wv", [D, D], F32, kind="ExternalInput").ap()
    wo_d = nc.dram_tensor("wo", [D, D], F32, kind="ExternalInput").ap()
    rw_d = nc.dram_tensor("rw", [D, NEXP], F32, kind="ExternalInput").ap()
    rb_d = nc.dram_tensor("rb", [NEXP, 1], F32, kind="ExternalInput").ap()
    w1_d = nc.dram_tensor("w1", [NEXP, D, DFF], F32, kind="ExternalInput").ap()
    b1_d = nc.dram_tensor("b1", [128, NEXP, 32], F32, kind="ExternalInput").ap()
    w2_d = nc.dram_tensor("w2", [NEXP, DFF, D], F32, kind="ExternalInput").ap()
    b2_d = nc.dram_tensor("b2", [NEXP, D], F32, kind="ExternalInput").ap()
    id8_d = nc.dram_tensor("id8", [8, 8], F32, kind="ExternalInput").ap()
    iota_d = nc.dram_tensor("iota16", [128, nb * 16], I16, kind="ExternalInput").ap()
    pos_d = nc.dram_tensor("pos16", [16, CAP // 16], F32, kind="ExternalInput").ap()
    esel_d = nc.dram_tensor("esel", [1, NEXP], F32, kind="ExternalInput").ap()

    y_d = nc.dram_tensor("y", [ntok + 128, D], F32, kind="ExternalOutput").ap()
    dbg = {}
    if debug:
        dbg["xnT"] = nc.dram_tensor("dbg_xnT", [128, 8, S], BF16, kind="ExternalOutput").ap()
        dbg["sc"] = nc.dram_tensor("dbg_sc", [128, nb * 16, 8], F32, kind="ExternalOutput").ap()
        dbg["tau"] = nc.dram_tensor("dbg_tau", [128, 8], F32, kind="ExternalOutput").ap()
        dbg["nf"] = nc.dram_tensor("dbg_nf", [1, NEXP], U32, kind="ExternalOutput").ap()
        if nb == 1:
            dbg["qt"] = nc.dram_tensor("dbg_qt", [128, S], BF16, kind="ExternalOutput").ap()
            dbg["kt"] = nc.dram_tensor("dbg_kt", [128, S], BF16, kind="ExternalOutput").ap()
            dbg["vaug"] = nc.dram_tensor("dbg_vaug", [128, 16, 80], BF16, kind="ExternalOutput").ap()
            dbg["attnT"] = nc.dram_tensor("dbg_attnT", [128, 8, S], BF16, kind="ExternalOutput").ap()
            dbg["lrow"] = nc.dram_tensor("dbg_lrow", [1, 4096], F32, kind="ExternalOutput").ap()
            dbg["ln"] = nc.dram_tensor("dbg_ln", [128, 8], F32, kind="ExternalOutput").ap()
            dbg["lnr16"] = nc.dram_tensor("dbg_lnr16", [128, 16], F32, kind="ExternalOutput").ap()
            dbg["lnm16"] = nc.dram_tensor("dbg_lnm16", [128, 16], F32, kind="ExternalOutput").ap()
            dbg["lnv16"] = nc.dram_tensor("dbg_lnv16", [128, 16], F32, kind="ExternalOutput").ap()
            dbg["xn0"] = nc.dram_tensor("dbg_xn0", [128, D], BF16, kind="ExternalOutput").ap()

    xt_tab = nc.dram_tensor("xt_tab", [ntok + 128, D], BF16)
    sct_tab = nc.dram_tensor("sct_tab", [ntok + 128, 64], F32)
    cc_in = nc.dram_tensor("cc_in", [128, nb * 16, 8], F32)
    cc_out = nc.dram_tensor("cc_out", [8, 128, nb * 16, 8], F32)
    cc_tin = nc.dram_tensor("cc_tin", [1, 1], F32)
    cc_tout = nc.dram_tensor("cc_tout", [8, 1], F32)

    y_rows = y_d[0:ntok, :].rearrange("(b p s) d -> b p s d", b=nb, p=128)
    xt_rows = xt_tab.ap()[0:ntok, :].rearrange("(b p s) d -> b p s d", b=nb, p=128)
    sct_rows = sct_tab.ap()[0:ntok, :].rearrange("(b p s) d -> b p s d", b=nb, p=128)

    ctx = ExitStack()
    with ctx:
        ARENA_BYTES = 206 * 1024
        arena = ctx.enter_context(nc.sbuf_tensor("arena", [128, ARENA_BYTES], U8))
        cursor = [0]

        def carve(nbytes, align=128):
            off = (cursor[0] + align - 1) // align * align
            cursor[0] = off + nbytes
            pass
            return off

        def view(off, shape, dt_):
            esz = {F32: 4, BF16: 2, I16: 2, U8: 1, U32: 4}[dt_]
            n = int(np.prod(shape[1:])) * esz
            ap = arena[0:shape[0], off:off + n].bitcast(dt_)
            if len(shape) == 2:
                return ap
            if len(shape) == 3:
                return ap.rearrange("p (a b) -> p a b", a=shape[1])
            if len(shape) == 4:
                return ap.rearrange("p (a b c) -> p a b c", a=shape[1], b=shape[2])
            if len(shape) == 5:
                return ap.rearrange("p (a b c d) -> p a b c d",
                                    a=shape[1], b=shape[2], c=shape[3])
            raise ValueError

        import os as _os
        def alloc(shape, dt_, _dbg=[0]):
            esz = {F32: 4, BF16: 2, I16: 2, U8: 1, U32: 4}[dt_]
            off = carve(int(np.prod(shape[1:])) * esz)
            if _os.environ.get("ARENA_DBG"):
                print("alloc", shape, dt_, "->", off, cursor[0])
            return view(off, shape, dt_), off

        iota_s, _ = alloc([128, nb * 16], I16)
        pos_s, _ = alloc([16, CAP // 16], F32)
        id8_s, _ = alloc([8, 8], F32)
        rb_s, _ = alloc([8, 1], F32)
        b1_s, _ = alloc([128, NEXP, 32], F32)
        scores_all, _ = alloc([128, nb * 16, 8], F32)
        idxs = []
        for e in range(NEXP):
            t, _ = alloc([128, CAP // 16], I16)
            idxs.append(t)
        nf_s, _ = alloc([1, NEXP], U32)
        kth_s, _ = alloc([1, 16], F32)
        tau_b, _ = alloc([128, 8], F32)
        esel_s, _ = alloc([1, NEXP], F32)
        esel_b, _ = alloc([128, NEXP], F32)
        tau8, _ = alloc([1, 8], F32)
        ones64, _ = alloc([1, 64], F32)
        epst, _ = alloc([128, 1], F32)
        if debug:
            dbgln, _ = alloc([128, 8], F32)
            dbgr16, _ = alloc([128, 16], F32)
            dbgm16, _ = alloc([128, 16], F32)
            dbgv16, _ = alloc([128, 16], F32)

        base = cursor[0]

        wv_s, _ = alloc([128, 8, D], BF16)
        wo_s, _ = alloc([128, 8, D], BF16)
        wqk, _ = alloc([128, 2, 2, 8, 128], BF16)
        rw_s, _ = alloc([128, 8, 8], BF16)
        xnT, o_xnT = alloc([128, 8, S], BF16)
        attnT, o_attnT = alloc([128, 8, S], BF16)
        qt, o_qt = alloc([128, 2, S], BF16)
        kt_, _ = alloc([128, 2, S], BF16)
        vt, _ = alloc([128, S], BF16)
        vaug0, _ = alloc([128, 2, 16, 80], BF16)
        vaug1, _ = alloc([128, 2, 16, 80], BF16)
        pt, _ = alloc([128, 2, 1024], BF16)
        xt3, _ = alloc([128, 3, D], F32)
        x1t, _ = alloc([128, 2, D], F32)
        xn2b, _ = alloc([128, 2, D], BF16)
        stg_odd, o_stg = alloc([64, 2, 1024], BF16)
        sqd = pt[:, 0, :]
        ssum, _ = alloc([128, 2], F32)
        sumsq, _ = alloc([128, 2], F32)
        lnm, _ = alloc([128, 2], F32)
        lnv, _ = alloc([128, 2], F32)
        lnr, _ = alloc([128, 2], F32)
        sub = cursor[0]
        Lq = arena[64:65, o_stg:o_stg + 4096 * 4].bitcast(F32)
        linv, _ = alloc([1, 4096], F32)
        Lp2, _ = alloc([128, 32], F32)
        lb_sb, _ = alloc([64, 1024], F32)
        l_end = cursor[0]
        cursor[0] = sub
        lg_s, _ = alloc([8, S], F32)
        se_t, _ = alloc([128, 2, 8], F32)
        sden, _ = alloc([128, 2], F32)
        cursor[0] = max(cursor[0], l_end)
        attn_end = cursor[0]

        cursor[0] = base
        scg, _ = alloc([128, 512, 8], F32)
        w1c, o_w1c = alloc([128, 2, 8, 1024], BF16)
        w2c, _ = alloc([128, 2, 8, 1024], BF16)
        h1T, _ = alloc([128, 32, CAP], BF16)
        gt_s, _ = alloc([128, NCH, 8, 128], BF16)
        gsc_s, _ = alloc([128, NCH, 64], F32)
        contrib, _ = alloc([128, NCH, D], F32)
        b2b, _ = alloc([128, D], F32)
        b2row, _ = alloc([1, D], F32)
        msk8, _ = alloc([128, nb * 16], U8)
        sel16, _ = alloc([128, nb * 16], I16)
        neg16, _ = alloc([128, nb * 16], I16)
        selR, _ = alloc([16, nb * 128], I16)
        nff, _ = alloc([1, 1], F32)
        nfb, _ = alloc([16, 1], F32)
        pmask, _ = alloc([16, CAP // 16], U8)
        idum, _ = alloc([16, CAP // 16], I16)
        my_sc = view(o_w1c, [128, 512], F32)
        mskf = view(o_w1c + 2048, [128, 512], F32)
        bis = view(o_w1c + 4096, [128, 8], F32)
        cntp = view(o_w1c + 4160, [128, 1], F32)
        onem = view(o_w1c + 4224, [128, 128], F32)
        ffn_end = cursor[0]
        print(f"arena: attn={attn_end} ffn={ffn_end} cap={ARENA_BYTES}")
        assert max(attn_end, ffn_end) <= ARENA_BYTES

        ps_a = ctx.enter_context(nc.psum_tensor("ps_a", [128, 2048], F32))
        ps_b = ctx.enter_context(nc.psum_tensor("ps_b", [128, 2048], F32))

        sem_names = ["d_g", "d_s", "d_sy", "d_st", "c_v", "c_s", "c_t", "c_g", "c_cc"]
        sems = {n: ctx.enter_context(nc.semaphore(n)) for n in sem_names}
        blk = ctx.enter_context(nc.Block())

        def program(eng, E):
            C = {n: 0 for n in sem_names}
            M = {}

            def me(name):
                return eng == name

            def w(sem, val):
                E.wait_ge(sems[sem], val)

            def inc(inst, sem, n=1):
                if inst is not None:
                    inst.then_inc(sems[sem], n)
                C[sem] += n
                return C[sem]

            def dma(q, out, in_, sem, transpose=False, slow=False):
                i = (E.dma_start(out=out, in_=in_, transpose=transpose,
                                 allow_slow_non_contiguous=slow) if me(q) else None)
                return inc(i, sem, 16)

            for (dst, src) in ((wv_s, wv_d), (wo_s, wo_d)):
                dma("gpsimd", dst[:], src.rearrange("(ft p) d -> p ft d", p=128), "d_g")
            dma("gpsimd", rw_s[:], rw_d.rearrange("(ft p) e -> p ft e", p=128), "d_g")
            t_wload = dma("gpsimd", b1_s[:], b1_d, "d_g")
            dma("scalar", rb_s[:], rb_d, "d_s")
            dma("scalar", id8_s[:], id8_d, "d_s")
            dma("scalar", iota_s[:], iota_d, "d_s")
            dma("scalar", pos_s[:], pos_d, "d_s")
            t_cload = dma("scalar", esel_s[:], esel_d, "d_s")
            if me("vector"):
                E.memset(ones64[:], 1.0)
                E.memset(epst, EPS)
                E.memset(vaug0[:, :, :, 64:65], 1.0)
                E.memset(vaug1[:, :, :, 64:65], 1.0)
                i = E.memset(sqd, 0.0)
            else:
                i = None
            t_init = inc(i, "c_v")
            if me("scalar"):
                w("c_v", t_init)
            dma("scalar", xt_tab.ap()[ntok:ntok + 128, :], sqd, "d_s")
            t_ztab = dma("scalar", sct_tab.ap()[ntok:ntok + 128, :], sqd.bitcast(F32)[:, 0:64], "d_s")
            M_zt = t_ztab

            def emit_ln_stats(xsrc, lnslot):
                if me("vector"):
                    E.tensor_mul(sqd, xsrc, xsrc)
                    E.reduce_sum(sumsq[:, lnslot:lnslot + 1], sqd, axis=AX.X)
                    i = E.reduce_sum(ssum[:, lnslot:lnslot + 1], xsrc, axis=AX.X)
                else:
                    i = None
                t_red = inc(i, "c_v")
                if me("vector"):
                    w("c_v", t_red)
                    if ("lnv_free", lnslot) in M:
                        w("c_s", M[("lnv_free", lnslot)])
                    i = E.tensor_scalar_mul(lnm[:, lnslot:lnslot + 1], ssum[:, lnslot:lnslot + 1], 1.0 / D)
                else:
                    i = None
                tt1 = inc(i, "c_v")
                if me("vector"):
                    w("c_v", tt1)
                    i = E.tensor_mul(lnv[:, lnslot:lnslot + 1], lnm[:, lnslot:lnslot + 1], lnm[:, lnslot:lnslot + 1])
                else:
                    i = None
                tt2 = inc(i, "c_v")
                if me("vector"):
                    w("c_v", tt2)
                    i = E.scalar_tensor_tensor(lnv[:, lnslot:lnslot + 1], sumsq[:, lnslot:lnslot + 1],
                                               1.0 / D, lnv[:, lnslot:lnslot + 1], ALU.mult, ALU.subtract)
                else:
                    i = None
                t_stats = inc(i, "c_v")
                if me("scalar"):
                    w("c_v", t_stats)
                    E.activation(lnr[:, lnslot:lnslot + 1], lnv[:, lnslot:lnslot + 1], AF.Ln, bias=epst)
                    i = E.activation(lnr[:, lnslot:lnslot + 1], lnr[:, lnslot:lnslot + 1], AF.Exp, scale=-0.5)
                else:
                    i = None
                t_r = inc(i, "c_s")
                M[("lnv_free", lnslot)] = t_r
                return t_r

            def emit_ln_apply(xsrc, lnslot, xn_dst, t_r):
                if me("vector"):
                    w("c_s", t_r)
                    i = E.tensor_scalar(xn_dst, xsrc, lnm[:, lnslot:lnslot + 1],
                                        lnr[:, lnslot:lnslot + 1], ALU.subtract, ALU.mult)
                else:
                    i = None
                return inc(i, "c_v")

            def emit_ln(xsrc, lnslot, xn_dst):
                return emit_ln_apply(xsrc, lnslot, xn_dst, emit_ln_stats(xsrc, lnslot))

            def emit_division(hp):
                t_all_ev = M[("pair_ev", hp)]
                if me("sync"):
                    w("c_v", t_all_ev)
                tok = dma("sync", Lp2[:], Lq[0:1, :], "d_sy")
                if me("vector"):
                    w("d_sy", tok)
                    i = E.reciprocal(Lp2[:], Lp2[:])
                else:
                    i = None
                t_rc = inc(i, "c_v")
                if me("sync"):
                    w("c_v", t_rc)
                t_linv = dma("sync", linv[:], Lp2[:], "d_sy")
                for u in range(4):
                    hh, qc = u // 2, u % 2
                    if me("gpsimd"):
                        w("d_sy", t_linv)
                        if "lbg" in M:
                            w("c_v", M["lbg"])
                        i = E.partition_broadcast(lb_sb[:],
                                                  linv[0:1, u * 1024:(u + 1) * 1024], 64)
                    else:
                        i = None
                    t_lb = inc(i, "c_g")
                    if me("vector"):
                        w("c_g", t_lb)
                        if hh == 0:
                            sl = attnT[0:64, hp, qc * 1024:(qc + 1) * 1024]
                        else:
                            sl = stg_odd[:, qc, :]
                        i = E.tensor_mul(sl, sl, lb_sb[:])
                    else:
                        i = None
                    t_div = inc(i, "c_v")
                    M["lbg"] = t_div
                M["ot_ev"] = max(M.get("ot_ev", 0), t_div)
                if me("sync"):
                    w("c_v", t_div)
                for qc in range(2):
                    dma("sync", attnT[64:128, hp, qc * 1024:(qc + 1) * 1024], stg_odd[:, qc, :], "d_sy")
                M[("div_done", hp)] = t_div
                M["stg_free"] = C["d_sy"]

            for b in range(nb):
                ln_tr = {}
                for s in range(NSL + 1):
                    if s < NSL:
                        xslot, lnslot = s % 3, s % 2
                        if me("scalar") and ("x_free", xslot) in M:
                            w("c_v", M[("x_free", xslot)])
                        tok = dma("scalar", xt3[:, xslot, :], x_in[b, s * 128:(s + 1) * 128, :], "d_s")
                        if me("vector"):
                            w("d_s", max(tok, M_zt))
                        ln_tr[s] = emit_ln_stats(xt3[:, xslot, :], lnslot)
                    if s >= 1:
                        sp = s - 1
                        xslot, lnslot = sp % 3, sp % 2
                        if me("vector") and ("xn_free", lnslot) in M:
                            w("d_sy", M[("xn_free", lnslot)])
                        t_xn = emit_ln_apply(xt3[:, xslot, :], lnslot, xn2b[:, lnslot, :], ln_tr[sp])
                        M[("x_free", xslot)] = t_xn
                        if me("sync"):
                            w("c_v", t_xn)
                        dma("sync", xnT[:, :, sp * 128:(sp + 1) * 128],
                            xn2b[:, lnslot, :], "d_sy", transpose=True)
                        M[("xn_free", lnslot)] = C["d_sy"]
                t_xnT = C["d_sy"]
                if debug and b == 0 and nb == 1:
                    if me("sync"):
                        w("d_sy", t_xnT)
                        w("c_v", C["c_v"])
                    dma("sync", dbg["xnT"], xnT[:], "d_st")
                    dma("sync", dbg["lnr16"], dbgr16[:], "d_st")
                    dma("sync", dbg["lnm16"], dbgm16[:], "d_st")
                    dma("sync", dbg["lnv16"], dbgv16[:], "d_st")

                def emit_wqk_load(hp, slot):
                    if me("gpsimd") and ("qk_ready_slot", slot) in M:
                        w("c_v", M[("qk_ready_slot", slot)])
                    for mi, wsrc in enumerate((wq_d, wk_d)):
                        tok = dma("gpsimd", wqk[:, mi, slot],
                                  wsrc.rearrange("(ft p) d -> p ft d", p=128)[:, :, hp * 128:(hp + 1) * 128],
                                  "d_g")
                    M[("wqk_tok", hp)] = tok

                def emit_qkv_group(hp, cg):
                    slot = hp % 2
                    psl = cg % 2
                    ch = cg % 4
                    if cg < 4:
                        dst_sl = vt[:, ch * 512:(ch + 1) * 512]
                        lhs = lambda ft: wv_s[:, ft, hp * 128:(hp + 1) * 128]
                    elif cg < 8:
                        dst_sl = qt[:, slot, ch * 512:(ch + 1) * 512]
                        lhs = lambda ft: wqk[:, 0, slot, ft, :]
                    else:
                        dst_sl = kt_[:, slot, ch * 512:(ch + 1) * 512]
                        lhs = lambda ft: wqk[:, 1, slot, ft, :]
                    ps = ps_b[:, 1024 + psl * 512:1024 + (psl + 1) * 512]
                    i = None
                    for ft in range(8):
                        if me("tensor"):
                            if ft == 0:
                                w("d_sy", t_xnT)
                                if cg >= 4:
                                    w("d_g", M[("wqk_tok", hp)])
                                if ("qkv_ps", psl) in M:
                                    w("c_v", M[("qkv_ps", psl)])
                            i = E.matmul(ps, lhs(ft), xnT[:, ft, ch * 512:(ch + 1) * 512],
                                         start=(ft == 0), stop=(ft == 7))
                    t_mm = inc(i, "c_t")
                    if me("vector"):
                        w("c_t", t_mm)
                        if cg == 0 and "vt_xp" in M:
                            w("d_sy", M["vt_xp"])
                        i = E.tensor_copy(dst_sl, ps)
                    else:
                        i = None
                    tok = inc(i, "c_v")
                    M[("qkv_ps", psl)] = tok
                    if cg == 3:
                        M[("vt_done", hp)] = tok
                    if cg == 11:
                        M[("qk_ready", hp)] = tok
                        M[("qk_ready_slot", slot)] = tok

                def emit_v_transpose(hp):
                    slot = hp % 2
                    if me("sync"):
                        w("c_v", M[("vt_done", hp)])
                    dma("sync", vaug0[:, slot, :, 0:64], vt[0:64, :], "d_sy", transpose=True)
                    tok = dma("sync", vaug1[:, slot, :, 0:64], vt[64:128, :], "d_sy", transpose=True)
                    M[("vaug_rdy", hp)] = tok
                    M["vt_xp"] = tok

                def emit_sdp_unit(hp, hh, qc, step, st_):
                    slot = hp % 2
                    hbase = 64 * hh
                    va = (vaug0 if hh == 0 else vaug1)
                    if step < 16:
                        ktile = step
                        stp = ktile % 2
                        i = None
                        for n in range(2):
                            if me("tensor"):
                                if n == 0:
                                    if hh == 0 and qc == 0 and ktile == 0:
                                        w("c_v", M[("qk_ready", hp)])
                                    if ("exp", stp) in M:
                                        w("c_s", M[("exp", stp)])
                                i = E.matmul(ps_a[:, stp * 1024 + n * 512: stp * 1024 + (n + 1) * 512],
                                             kt_[hbase:hbase + 64, slot, ktile * 128:(ktile + 1) * 128],
                                             qt[hbase:hbase + 64, slot, qc * 1024 + n * 512: qc * 1024 + (n + 1) * 512],
                                             start=True, stop=True)
                        t_st = inc(i, "c_t")
                        if me("scalar"):
                            w("c_t", t_st)
                            if ("pt_free", stp) in M:
                                w("c_t", M[("pt_free", stp)])
                            i = E.activation(pt[:, stp, :], ps_a[:, stp * 1024:(stp + 1) * 1024],
                                             AF.Exp, scale=SCALE)
                        else:
                            i = None
                        t_exp = inc(i, "c_s")
                        M[("exp", stp)] = t_exp
                        st_[ktile] = t_exp
                        if ktile >= 14:
                            M["st_free"] = t_exp
                    else:
                        pass
                    if step >= 1:
                        ktile = step - 1
                        stp = ktile % 2
                        i = None
                        for n in range(2):
                            if me("tensor"):
                                if n == 0:
                                    w("c_s", st_[ktile])
                                    if hh == 0 and qc == 0 and ktile == 0:
                                        w("d_sy", M[("vaug_rdy", hp)])
                                    if ktile == 0 and "ot_ev" in M:
                                        w("c_v", M["ot_ev"])
                                i = E.matmul(ps_b[0:65, n * 512:(n + 1) * 512],
                                             va[:, slot, ktile, 0:65],
                                             pt[:, stp, n * 512:(n + 1) * 512],
                                             start=(ktile == 0), stop=(ktile == 15))
                        t_pv = inc(i, "c_t")
                        M[("pt_free", stp)] = t_pv
                        if step == 16:
                            u = hh * 2 + qc
                            if me("vector"):
                                w("c_t", t_pv)
                                if hh == 1 and "stg_free" in M:
                                    w("d_sy", M["stg_free"])
                                E.tensor_copy(Lq[0:1, u * 1024:(u + 1) * 1024],
                                              ps_b[64:65, 0:1024])
                                if hh == 0:
                                    i = E.tensor_copy(attnT[0:64, hp, qc * 1024:(qc + 1) * 1024],
                                                      ps_b[0:64, 0:1024])
                                else:
                                    i = E.tensor_copy(stg_odd[:, qc, :],
                                                      ps_b[0:64, 0:1024])
                            else:
                                i = None
                            M["ot_ev"] = inc(i, "c_v")
                            if u == 3:
                                M[("pair_ev", hp)] = M["ot_ev"]

                emit_wqk_load(0, 0)
                for hp in range(8):
                    if hp == 0:
                        for cg in range(12):
                            emit_qkv_group(0, cg)
                        emit_v_transpose(0)
                        emit_wqk_load(1, 1)
                    if hp > 0:
                        emit_division(hp - 1)
                    sdp_seq = []
                    for hh in range(2):
                        for qc in range(2):
                            st_ = {}
                            for step in range(17):
                                sdp_seq.append((hh, qc, step, st_))
                    qkv_seq = list(range(12)) if hp < 7 else []
                    qpos = 0
                    for idx, (hh, qc, step, st_) in enumerate(sdp_seq):
                        emit_sdp_unit(hp, hh, qc, step, st_)
                        want = (idx + 1) * 12 // len(sdp_seq)
                        while qpos < min(want, len(qkv_seq)):
                            emit_qkv_group(hp + 1, qkv_seq[qpos])
                            if qkv_seq[qpos] == 3:
                                emit_v_transpose(hp + 1)
                            qpos += 1
                    if hp < 7:
                        while qpos < 12:
                            emit_qkv_group(hp + 1, qkv_seq[qpos])
                            if qkv_seq[qpos] == 3:
                                emit_v_transpose(hp + 1)
                            qpos += 1
                        if hp < 6:
                            emit_wqk_load(hp + 2, hp % 2)
                emit_division(7)
                if debug and nb == 1 and b == 0:
                    tok = C["d_sy"]
                    if me("sync"):
                        w("d_sy", tok)
                    dma("sync", dbg["attnT"], attnT[:], "d_st")
                    dma("sync", dbg["lrow"], Lq[0:1, :], "d_st")

                t_attnT = C["d_sy"]
                for s in range(NSL):
                    xslot, lnslot = s % 2, s % 2
                    if me("scalar") and ("x_free", xslot) in M:
                        w("c_v", M[("x_free", xslot)])
                    tok_x = dma("scalar", xt3[:, xslot, :], x_in[b, s * 128:(s + 1) * 128, :], "d_s")
                    if me("tensor"):
                        w("d_sy", t_attnT)
                        if ("proj_free", s % 2) in M:
                            w("c_v", M[("proj_free", s % 2)])
                    i = None
                    for ft in range(8):
                        for ch in range(2):
                            if me("tensor"):
                                i = E.matmul(ps_a[:, (s % 2) * 1024 + ch * 512:(s % 2) * 1024 + (ch + 1) * 512],
                                             attnT[:, ft, s * 128:(s + 1) * 128],
                                             wo_s[:, ft, ch * 512:(ch + 1) * 512],
                                             start=(ft == 0), stop=(ft == 7))
                    t_proj = inc(i, "c_t")
                    if me("vector"):
                        w("c_t", t_proj)
                        w("d_s", tok_x)
                        if ("x1_free", s % 2) in M:
                            w("d_st", M[("x1_free", s % 2)])
                        i = E.tensor_add(x1t[:, s % 2, :], xt3[:, xslot, :],
                                         ps_a[:, (s % 2) * 1024:(s % 2) * 1024 + 1024])
                    else:
                        i = None
                    t_x1 = inc(i, "c_v")
                    M[("proj_free", s % 2)] = t_x1
                    if me("scalar"):
                        w("c_v", t_x1)
                    tok_y = dma("scalar", y_rows[b, :, s, :], x1t[:, s % 2, :], "d_st")
                    if me("vector"):
                        if ("xn_free", lnslot) in M:
                            w("d_sy", M[("xn_free", lnslot)])
                        if ("xn2_free", lnslot) in M:
                            w("d_st", M[("xn2_free", lnslot)])
                    t_xn = emit_ln(x1t[:, s % 2, :], lnslot, xn2b[:, lnslot, :])
                    M[("x_free", xslot)] = t_xn
                    M[("x1_free", s % 2)] = tok_y
                    if me("scalar"):
                        w("c_v", t_xn)
                    dma("scalar", xt_rows[b, :, s, :], xn2b[:, lnslot, :], "d_st")
                    M[("xn2_free", lnslot)] = C["d_st"]
                    if me("sync"):
                        w("c_v", t_xn)
                    dma("sync", xnT[:, :, s * 128:(s + 1) * 128],
                        xn2b[:, lnslot, :], "d_sy", transpose=True)
                    M[("xn_free", lnslot)] = C["d_sy"]

                t_xnT2 = C["d_sy"]
                if me("tensor"):
                    w("d_sy", t_xnT2)
                    w("c_v", M[("div_done", 7)])
                    if "lg_free" in M:
                        w("c_v", M["lg_free"])
                i = None
                for ft in range(8):
                    for ch in range(4):
                        if me("tensor"):
                            i = E.matmul(ps_b[0:8, ch * 512:(ch + 1) * 512],
                                         rw_s[:, ft, 0:8],
                                         xnT[:, ft, ch * 512:(ch + 1) * 512],
                                         start=(ft == 0), stop=(ft == 7))
                t_lg = inc(i, "c_t")
                if me("vector"):
                    w("c_t", t_lg)
                    w("d_s", t_cload)
                    i = E.tensor_scalar(lg_s[:], ps_b[0:8, :], rb_s[0:8, 0:1], None, ALU.add)
                else:
                    i = None
                t_lgs = inc(i, "c_v")
                M["lg_free"] = t_lgs
                for s in range(NSL):
                    if me("tensor"):
                        w("c_v", t_lgs)
                        if ("se_free", s % 2) in M:
                            w("c_s", M[("se_free", s % 2)])
                        if ("sc_free", s % 2) in M:
                            w("c_v", M[("sc_free", s % 2)])
                        i = E.transpose(ps_a[:, (s % 2) * 512:(s % 2) * 512 + 8],
                                        lg_s[0:8, s * 128:(s + 1) * 128], id8_s[:])
                    else:
                        i = None
                    t_tp = inc(i, "c_t")
                    if me("scalar"):
                        w("c_t", t_tp)
                        if ("sc_free", s % 2) in M:
                            w("c_v", M[("sc_free", s % 2)])
                        i = E.activation(se_t[:, s % 2, :], ps_a[:, (s % 2) * 512:(s % 2) * 512 + 8],
                                         AF.Exp, accum_out=sden[:, s % 2:s % 2 + 1])
                    else:
                        i = None
                    t_se = inc(i, "c_s")
                    M[("se_free", s % 2)] = t_se
                    if me("vector"):
                        w("c_s", t_se)
                        i = E.reciprocal(sden[:, s % 2:s % 2 + 1], sden[:, s % 2:s % 2 + 1])
                    else:
                        i = None
                    t_rd = inc(i, "c_v")
                    if me("vector"):
                        w("c_v", t_rd)
                        i = E.tensor_scalar_mul(scores_all[:, b * 16 + s, :], se_t[:, s % 2, :],
                                                sden[:, s % 2:s % 2 + 1])
                    else:
                        i = None
                    t_sc = inc(i, "c_v")
                    M[("sc_free", s % 2)] = t_sc
                    M["router_ps_free"] = t_se
                t_scb = C["c_v"]
                M["ot_ev"] = max(M.get("ot_ev", 0), t_lgs)
                M[("qkv_ps", 0)] = max(M.get(("qkv_ps", 0), 0), t_lgs)
                M[("qkv_ps", 1)] = max(M.get(("qkv_ps", 1), 0), t_lgs)
                M[("exp", 0)] = max(M.get(("exp", 0), 0), C["c_s"])
                M[("exp", 1)] = max(M.get(("exp", 1), 0), C["c_s"])
                if me("scalar"):
                    w("c_v", t_scb)
                dma("scalar", sct_rows[b][:, :, 0:8], scores_all[:, b * 16:(b + 1) * 16, :], "d_st")
                dma("scalar", cc_in.ap()[:, b * 16:(b + 1) * 16, :],
                    scores_all[:, b * 16:(b + 1) * 16, :], "d_st")

            if me("vector"):
                w("d_st", C["d_st"])
                i = E.memset(x1t[:, 0, :], 0.0)
            else:
                i = None
            tz = inc(i, "c_v")
            if me("scalar"):
                w("c_v", tz)
            dma("scalar", y_d[ntok:ntok + 128, :], x1t[:, 0, :], "d_st")

            if debug:
                tok = C["c_v"]
                if me("scalar"):
                    w("c_v", tok)
                dma("scalar", dbg["sc"], scores_all[:], "d_st")

            if not do_ffn:
                if me("sync"):
                    for sname in ("d_st", "d_s", "d_g", "d_sy"):
                        w(sname, C[sname])
                return

            t_ccin = C["d_st"]
            if me("gpsimd"):
                w("d_st", t_ccin)
                i = E.collective_compute("AllGather", ALU.bypass,
                                         replica_groups=[list(range(8))],
                                         ins=[cc_in.ap().opt()], outs=[cc_out.ap().opt()])
            else:
                i = None
            t_cc = inc(i, "c_cc")
            if me("scalar"):
                w("c_cc", t_cc)
            for r in range(8):
                dma("scalar", scg[:, r * (nb * 16):(r + 1) * (nb * 16), :], cc_out.ap()[r], "d_s")
            t_scg = C["d_s"]

            if me("gpsimd"):
                w("d_s", t_cload)
                i = E.partition_broadcast(esel_b[:], esel_s[0:1, :], 128)
            else:
                i = None
            t_eb = inc(i, "c_g")
            if me("vector"):
                w("c_g", t_eb)
                w("d_s", t_scg)
                i = E.tensor_scalar_mul(my_sc[:], scg[:, :, 0:1].rearrange("p n o -> p (n o)"),
                                        esel_b[:, 0:1])
            else:
                i = None
            tok = inc(i, "c_v")
            for e in range(1, NEXP):
                if me("vector"):
                    i = E.scalar_tensor_tensor(my_sc[:], scg[:, :, e:e + 1].rearrange("p n o -> p (n o)"),
                                               esel_b[:, e:e + 1], my_sc[:], ALU.mult, ALU.add)
                else:
                    i = None
                tok = inc(i, "c_v")
            t_mysc = tok
            if me("vector"):
                w("c_v", t_mysc)
                E.memset(bis[:, 0:1], 0.0)
                E.memset(bis[:, 1:2], 1.0)
                E.memset(bis[:, 2:3], 0.5)
                i = E.memset(onem[:], 1.0)
            else:
                i = None
            t_bis = inc(i, "c_v")
            for it in range(36):
                if me("vector"):
                    w("c_v", t_bis)
                    E.tensor_scalar(mskf[:], my_sc[:], bis[:, 2:3], None, ALU.is_gt)
                    i = E.reduce_sum(cntp[:], mskf[:], axis=AX.X)
                else:
                    i = None
                t_cnt = inc(i, "c_v")
                if me("tensor"):
                    w("c_v", t_cnt)
                    i = E.matmul(ps_b[0:128, (it % 2) * 512:(it % 2) * 512 + 1],
                                 onem[:], cntp[:], start=True, stop=True)
                else:
                    i = None
                t_tot = inc(i, "c_t")
                if me("vector"):
                    w("c_t", t_tot)
                    i = E.tensor_scalar(bis[:, 4:5], ps_b[0:128, (it % 2) * 512:(it % 2) * 512 + 1],
                                        2048.0, None, ALU.is_ge)
                else:
                    i = None
                tb3 = inc(i, "c_v")
                if me("vector"):
                    w("c_v", tb3)
                    E.tensor_sub(bis[:, 5:6], bis[:, 2:3], bis[:, 0:1])
                    E.tensor_sub(bis[:, 6:7], bis[:, 1:2], bis[:, 2:3])
                    i = E.tensor_scalar(bis[:, 7:8], bis[:, 4:5], -1.0, 1.0, ALU.mult, ALU.add)
                else:
                    i = None
                tb4 = inc(i, "c_v")
                if me("vector"):
                    w("c_v", tb4)
                    E.tensor_mul(bis[:, 5:6], bis[:, 5:6], bis[:, 4:5])
                    i = E.tensor_mul(bis[:, 6:7], bis[:, 6:7], bis[:, 7:8])
                else:
                    i = None
                tb5 = inc(i, "c_v")
                if me("vector"):
                    w("c_v", tb5)
                    E.tensor_add(bis[:, 0:1], bis[:, 0:1], bis[:, 5:6])
                    i = E.tensor_sub(bis[:, 1:2], bis[:, 1:2], bis[:, 6:7])
                else:
                    i = None
                tb6 = inc(i, "c_v")
                if me("vector"):
                    w("c_v", tb6)
                    i = E.tensor_add(bis[:, 3:4], bis[:, 0:1], bis[:, 1:2])
                else:
                    i = None
                tb7 = inc(i, "c_v")
                if me("vector"):
                    w("c_v", tb7)
                    i = E.tensor_scalar_mul(bis[:, 2:3], bis[:, 3:4], 0.5)
                else:
                    i = None
                t_bis = inc(i, "c_v")
            if me("scalar"):
                w("c_v", t_bis)
            tok = dma("scalar", cc_tin.ap(), bis[0:1, 0:1], "d_s")
            if me("gpsimd"):
                w("d_s", tok)
                i = E.collective_compute("AllGather", ALU.bypass,
                                         replica_groups=[list(range(8))],
                                         ins=[cc_tin.ap().opt()], outs=[cc_tout.ap().opt()])
            else:
                i = None
            t_cc2 = inc(i, "c_cc")
            if me("scalar"):
                w("c_cc", t_cc2)
            tok = dma("scalar", tau8[:], cc_tout.ap().rearrange("e o -> o e"), "d_s")
            if me("gpsimd"):
                w("d_s", tok)
                i = E.partition_broadcast(tau_b[:], tau8[0:1, :], 128)
            else:
                i = None
            t_tau = inc(i, "c_g")
            if debug:
                if me("scalar"):
                    w("c_g", t_tau)
                dma("scalar", dbg["tau"], tau_b[:], "d_st")

            for e in range(NEXP):
                if me("vector"):
                    w("c_g", t_tau)
                    if e == 0:
                        E.memset(neg16[:], -1)
                        E.memset(idum[:], NT)
                    if "selR_free" in M:
                        w("c_g", M["selR_free"])
                    E.tensor_scalar(msk8[:], scores_all[:, :, e:e + 1].rearrange("p n o -> p (n o)"),
                                    tau_b[:, e:e + 1], None, ALU.is_gt)
                    E.memset(idxs[e][0:16, :], NT)
                    i = E.select(sel16[:], msk8[:], iota_s[:], neg16[:])
                else:
                    i = None
                t_sel = inc(i, "c_v")
                if me("sync"):
                    w("c_v", t_sel)
                for k in range(8):
                    dma("sync", selR[:, nb * 16 * k: nb * 16 * (k + 1)],
                        sel16[16 * k:16 * (k + 1), :], "d_sy")
                t_selR = C["d_sy"]
                if me("gpsimd"):
                    w("d_sy", t_selR)
                    i = E.sparse_gather(idxs[e][0:16, :], selR[:], num_found=nf_s[0:1, e:e + 1])
                else:
                    i = None
                t_sg = inc(i, "c_g")
                M["selR_free"] = t_sg
                if me("vector"):
                    w("c_g", t_sg)
                    i = E.tensor_copy(nff[:], nf_s[0:1, e:e + 1])
                else:
                    i = None
                t_nff = inc(i, "c_v")
                if me("gpsimd"):
                    w("c_v", t_nff)
                    i = E.partition_broadcast(nfb[:], nff[0:1, :], 16)
                else:
                    i = None
                t_nfb = inc(i, "c_g")
                if me("vector"):
                    w("c_g", t_nfb)
                    E.tensor_scalar(pmask[:], pos_s[:], nfb[:, 0:1], None, ALU.is_ge)
                    i = E.select(idxs[e][0:16, :], pmask[:], idum[:], idxs[e][0:16, :])
                else:
                    i = None
                t_fix = inc(i, "c_v")
                if me("sync"):
                    w("c_v", t_fix)
                tk = dma("sync", idxs[e][16:32, :], idxs[e][0:16, :], "d_sy")
                if me("sync"):
                    w("d_sy", tk)
                tk = dma("sync", idxs[e][32:64, :], idxs[e][0:32, :], "d_sy")
                if me("sync"):
                    w("d_sy", tk)
                tk = dma("sync", idxs[e][64:128, :], idxs[e][0:64, :], "d_sy")
                M[("idx_done", e)] = tk
            if debug:
                tok = C["c_g"]
                if me("scalar"):
                    w("c_g", tok)
                dma("scalar", dbg["nf"], nf_s[:], "d_st")

            t_y_x1 = C["d_st"]
            for e in range(NEXP):
                if me("gpsimd"):
                    w("d_sy", M[("idx_done", e)])
                    if "gt_free" in M:
                        w("c_v", M["gt_free"])
                for k in range(NCH):
                    i = (E.dma_gather(gt_s[:, k], xt_tab.ap(),
                                      idxs[e][:, 8 * k:8 * (k + 1)], 128, 128, 1024, transpose=True)
                         if me("gpsimd") else None)
                    inc(i, "d_g", 16)
                    i = (E.dma_gather(gsc_s[:, k:k + 1, :].bitcast(BF16), sct_tab.ap().bitcast(BF16),
                                      idxs[e][:, 8 * k:8 * (k + 1)], 128, 128, 128)
                         if me("gpsimd") else None)
                    inc(i, "d_g", 16)
                t_gath = C["d_g"]
                if me("scalar") and "b2_free" in M:
                    w("c_g", M["b2_free"])
                tok_b2 = dma("scalar", b2row[:], b2_d[e:e + 1, :], "d_s")
                if me("gpsimd"):
                    w("d_s", tok_b2)
                    if "b2b_free" in M:
                        w("c_v", M["b2b_free"])
                    i = E.partition_broadcast(b2b[:], b2row[0:1, :], 128)
                else:
                    i = None
                t_b2b = inc(i, "c_g")
                M["b2_free"] = t_b2b
                for dffc in range(4):
                    wslot = dffc % 2
                    if me("gpsimd") and ("w1_free", wslot) in M:
                        w("c_t", M[("w1_free", wslot)])
                    tok_w1 = dma("gpsimd", w1c[:, wslot, :, :],
                                 w1_d[e].rearrange("(ft p) f -> p ft f", p=128)[:, :, dffc * 1024:(dffc + 1) * 1024],
                                 "d_g")
                    for dt in range(8):
                        pslot = dt % 2
                        if me("tensor"):
                            if dffc == 0 and dt == 0:
                                w("d_g", t_gath)
                            w("d_g", tok_w1)
                            if ("h1_free", pslot) in M:
                                w("c_v", M[("h1_free", pslot)])
                        i = None
                        for ft in range(8):
                            if me("tensor"):
                                i = E.matmul(ps_a[:, pslot * 512:pslot * 512 + CAP],
                                             w1c[:, wslot, ft, dt * 128:(dt + 1) * 128],
                                             gt_s[:, :, ft, :],
                                             start=(ft == 0), stop=(ft == 7))
                        t_h1 = inc(i, "c_t")
                        M[("w1_free", wslot)] = t_h1
                        if me("vector"):
                            w("c_t", t_h1)
                            i = E.tensor_scalar(h1T[:, dffc * 8 + dt, :], ps_a[:, pslot * 512:pslot * 512 + CAP],
                                                b1_s[:, e, dffc * 8 + dt:dffc * 8 + dt + 1], 0.0,
                                                ALU.add, ALU.max)
                        else:
                            i = None
                        M[("h1_free", pslot)] = inc(i, "c_v")
                t_h1all = C["c_v"]
                eo_slice = [ps_a[:, 0:1024], ps_a[:, 1024:2048], ps_b[:, 0:1024]]
                for dffc in range(4):
                    wslot = dffc % 2
                    if me("gpsimd") and ("w2_free", wslot) in M:
                        w("c_t", M[("w2_free", wslot)])
                    tok_w2 = dma("gpsimd", w2c[:, wslot, :, :],
                                 w2_d[e].rearrange("(c ft p) f -> c p ft f", p=128, ft=8)[dffc],
                                 "d_g")
                    i = None
                    for dt in range(8):
                        d32 = dffc * 8 + dt
                        for ts in range(NCH):
                            for ch in range(2):
                                if me("tensor"):
                                    if dffc == 0 and dt == 0 and ts == 0 and ch == 0:
                                        w("c_v", t_h1all)
                                        if "eo_free" in M:
                                            w("c_v", M["eo_free"])
                                    w("d_g", tok_w2)
                                    i = E.matmul(eo_slice[ts][:, ch * 512:(ch + 1) * 512],
                                                 h1T[:, d32, ts * 128:(ts + 1) * 128],
                                                 w2c[:, wslot, dt, ch * 512:(ch + 1) * 512],
                                                 start=(d32 == 0), stop=(d32 == 31))
                    t_eo = inc(i, "c_t")
                    M[("w2_free", wslot)] = t_eo
                for ts in range(NCH):
                    if me("vector"):
                        w("c_t", t_eo)
                        w("c_g", t_b2b)
                        w("d_g", t_gath)
                        if ("ctr_free", ts) in M:
                            w("d_g", M[("ctr_free", ts)])
                        E.tensor_add(contrib[:, ts, :], b2b[:], eo_slice[ts])
                        i = E.tensor_scalar_mul(contrib[:, ts, :], contrib[:, ts, :], gsc_s[:, ts, e:e + 1])
                    else:
                        i = None
                    inc(i, "c_v")
                t_ctr = C["c_v"]
                M["eo_free"] = t_ctr
                M["gt_free"] = t_ctr
                M["b2b_free"] = t_ctr
                if me("gpsimd"):
                    w("c_v", t_ctr)
                    if e == 0:
                        w("d_st", t_y_x1)
                    else:
                        w("d_g", M["scat_done"])
                for k in range(NCH):
                    i = (E.dma_scatter_add(y_d, contrib[:, k:k + 1, :],
                                           idxs[e][:, 8 * k:8 * (k + 1)], 128, 128, 1024)
                         if me("gpsimd") else None)
                    tk = inc(i, "d_g", 16)
                    M[("ctr_free", k)] = tk
                M["scat_done"] = C["d_g"]

            if me("sync"):
                for sname in ("d_st", "d_s", "d_g", "d_sy"):
                    w(sname, C[sname])

        for eng, starter in (("sync", blk.sync), ("scalar", blk.scalar), ("vector", blk.vector),
                             ("tensor", blk.tensor), ("gpsimd", blk.gpsimd)):
            starter(lambda E, eng=eng: program(eng, E))
    nc.finalize()
    return nc


def make_inputs(inp, core):
    iota = np.zeros((128, 64), np.int16)
    for p in range(128):
        for c in range(64):
            iota[p, c] = (c // 16) * 2048 + p * 16 + (c % 16)
    pos = (np.arange(16)[:, None] + np.arange(CAP // 16)[None, :] * 16).astype(np.float32)
    inp = {k: np.asarray(v) for k, v in inp.items()}
    return {
        "x": np.ascontiguousarray(inp["L0_attn_x"][core * NB:(core + 1) * NB], dtype=np.float32),
        "wq": inp["L0_attn_Wq"], "wk": inp["L0_attn_Wk"],
        "wv": inp["L0_attn_Wv"], "wo": inp["L0_attn_Wo"],
        "rw": inp["L0_ffn_router_w"], "rb": np.ascontiguousarray(inp["L0_ffn_router_b"].reshape(8, 1)),
        "w1": inp["L0_ffn_w1"],
        "b1": np.ascontiguousarray(np.asarray(inp["L0_ffn_b1"]).reshape(8, 32, 128).transpose(2, 0, 1)),
        "w2": inp["L0_ffn_w2"], "b2": inp["L0_ffn_b2"],
        "id8": np.eye(8, dtype=np.float32),
        "iota16": iota, "pos16": pos,
        "esel": np.eye(8, dtype=np.float32)[core:core + 1],
    }


def unshard_y(y):
    return y[:NT].reshape(NB, 128, 16, D).transpose(0, 2, 1, 3).reshape(NB, S, D)


_NC_CACHE = {}


def _get_nc():
    if "nc" not in _NC_CACHE:
        _NC_CACHE["nc"] = build(debug=False, do_ffn=True, nb=NB)
    return _NC_CACHE["nc"]


def kernel(**inputs):
    from concourse.bass_utils import run_bass_kernel_spmd
    nc = _get_nc()
    in_maps = [make_inputs(inputs, c) for c in range(8)]
    res = run_bass_kernel_spmd(nc, in_maps, core_ids=list(range(8)))
    y = np.concatenate([unshard_y(res.results[c]["y"]) for c in range(8)], 0)
    return y
